# version 03
# baseline (speedup 1.0000x reference)
import numpy as np
from contextlib import ExitStack

import concourse.bass as bass
import concourse.mybir as mybir
from concourse.bass_utils import run_bass_kernel_spmd

F32 = mybir.dt.float32
BF16 = mybir.dt.bfloat16
AF = mybir.ActivationFunctionType
ALU = mybir.AluOpType
AX = mybir.AxisListType

L = 4096
D = 2048
FP = 10304
DI = 4096
NH = 64
DH = 64
NS = 128
G = 8
CH = 256
NCH = L // CH
CONV = 6144
HFF = 5632
NE = 8
RK = 16
TOWN = 2048
EPS_RMS = 1e-6
EPS_LN = 1e-5

NDB = D // 128
NTB = L // 128
NTW = L // 512
NFB = (FP + 127) // 128

BUILD_UPTO = None
B2_CHUNKS = None
ONLY_PHASE = None
B2_STAGE = None


class StopBuild(Exception):
    pass


class Sem:
    def __init__(self, handle):
        self.h = handle
        self.n = 0

    def inc(self, amt):
        self.n += amt
        return self.n


class Blk:
    ENGINES = ("tensor", "vector", "scalar", "gpsimd", "sync")

    def __init__(self, nc):
        self.nc = nc
        self.ops = {e: [] for e in self.ENGINES}

    def on(self, eng, fn):
        self.ops[eng].append(fn)

    def emit(self):
        nc = self.nc
        with nc.Block() as block:
            for name in self.ENGINES:
                if not self.ops[name]:
                    continue
                def runner(e, _ops=tuple(self.ops[name])):
                    for f in _ops:
                        f(e)
                getattr(block, name)(runner)


def build_program(debug_taps=(), detect_races=True):
    nc = bass.Bass("TRN2", target_bir_lowering=False,
                   detect_race_conditions=detect_races)
    taps = {}

    x_tm = nc.dram_tensor("x_tm", [L, D], F32, kind="ExternalInput")
    x_fmT = nc.dram_tensor("x_fmT", [D, L], F32, kind="ExternalInput")
    ipW_T = nc.dram_tensor("ipW_T", [D, FP], F32, kind="ExternalInput")
    outW_T = nc.dram_tensor("outW_T", [DI, D], F32, kind="ExternalInput")
    cw_t = nc.dram_tensor("cw_t", [128, 192], F32, kind="ExternalInput")
    cb_t = nc.dram_tensor("cb_t", [128, 48], F32, kind="ExternalInput")
    cnw_t = nc.dram_tensor("cnw_t", [128, 48], F32, kind="ExternalInput")
    nw_t = nc.dram_tensor("nw_t", [128, 16], F32, kind="ExternalInput")
    onw_t = nc.dram_tensor("onw_t", [128, 32], F32, kind="ExternalInput")
    dtw_aug = nc.dram_tensor("dtw_aug", [65, 64], F32, kind="ExternalInput")
    alog_r = nc.dram_tensor("alog_r", [1, 64], F32, kind="ExternalInput")
    gwT = nc.dram_tensor("gwT", [D, NE], F32, kind="ExternalInput")
    upW_T = nc.dram_tensor("upW_T", [D, HFF], F32, kind="ExternalInput")
    gateW_T = nc.dram_tensor("gateW_T", [D, HFF], F32, kind="ExternalInput")
    downW_T = nc.dram_tensor("downW_T", [HFF, D], F32, kind="ExternalInput")
    upA_pk = nc.dram_tensor("upA_pk", [D, 256], F32, kind="ExternalInput")
    gateA_pk = nc.dram_tensor("gateA_pk", [D, 256], F32, kind="ExternalInput")
    upB_stk = nc.dram_tensor("upB_stk", [256, HFF], F32, kind="ExternalInput")
    gateB_stk = nc.dram_tensor("gateB_stk", [256, HFF], F32, kind="ExternalInput")
    dA_pk = nc.dram_tensor("dA_pk", [HFF, 256], F32, kind="ExternalInput")
    dB_pk = nc.dram_tensor("dB_pk", [256, D], F32, kind="ExternalInput")
    cpool = nc.dram_tensor("cpool", [128, 16], F32, kind="ExternalInput")
    ones_bf = nc.dram_tensor("ones_bf", [128, 128], BF16, kind="ExternalInput")
    ones_f = nc.dram_tensor("ones_f", [128, 128], F32, kind="ExternalInput")
    ident_f = nc.dram_tensor("ident_f", [128, 128], F32, kind="ExternalInput")
    triu_bf = nc.dram_tensor("triu_bf", [128, 128], BF16, kind="ExternalInput")
    triu_f = nc.dram_tensor("triu_f", [128, 128], F32, kind="ExternalInput")
    ownoff = nc.dram_tensor("ownoff", [1, 4], mybir.dt.int32, kind="ExternalInput")

    out_own = nc.dram_tensor("out_own", [TOWN, D], F32, kind="ExternalOutput")

    conv_s = nc.dram_tensor("conv_s", [CONV, L], BF16)
    gate_s = nc.dram_tensor("gate_s", [DI, L], BF16)
    convf_s = nc.dram_tensor("convf_s", [CONV, L], BF16)
    ygT_s = nc.dram_tensor("ygT_s", [DI, L], BF16)
    s1_s = nc.dram_tensor("s1_s", [L, D], F32)
    x2_s = nc.dram_tensor("x2_s", [L, D], F32)
    x2b_s = nc.dram_tensor("x2b_s", [L, D], BF16)
    woutb_s = nc.dram_tensor("woutb_s", [DI, D], BF16)
    mixed_s = nc.dram_tensor("mixed_s", [HFF, TOWN], BF16)
    uga_s = nc.dram_tensor("uga_s", [256, TOWN], BF16)
    dall_s = nc.dram_tensor("dall_s", [256, TOWN], BF16)
    hilo_s = nc.dram_tensor("hilo_s", [2, NH * CH], BF16)
    rrow_s = nc.dram_tensor("rrow_s", [1, L], F32)
    crow_s = nc.dram_tensor("crow_s", [NE, L], F32)
    x2own_s = nc.dram_tensor("x2own_s", [TOWN, D], BF16)
    x2owf_s = nc.dram_tensor("x2owf_s", [TOWN, D], F32)
    crown_s = nc.dram_tensor("crown_s", [NE, TOWN], F32)
    s2_s = nc.dram_tensor("s2_s", [TOWN, D], F32)

    def tap(name, shape, dtype=F32):
        if name in debug_taps:
            taps[name] = nc.dram_tensor("tap_" + name, shape, dtype, kind="ExternalOutput")
            return taps[name]
        return None

    def phase_done(name):
        return BUILD_UPTO is not None and BUILD_UPTO == name

    with ExitStack() as ctx:
        E = ctx.enter_context

        ALL_SEMS = []

        def mksem(name):
            s = Sem(E(nc.semaphore(name)))
            ALL_SEMS.append(s)
            return s

        sD = mksem("sD")
        sG = mksem("sG")
        sC = mksem("sC")
        sT = mksem("sT")
        sV = mksem("sV")
        sS = mksem("sS")
        sP = mksem("sP")
        sXF = [mksem(f"sXF{i}") for i in range(4)]
        sWA = [mksem(f"sWA{i}") for i in range(2)]
        sEV = [mksem(f"sEV{i}") for i in range(2)]
        sXC = mksem("sXC")
        sBT = mksem("sBT")
        sGT = mksem("sGT")
        sYT = mksem("sYT")
        sYG = mksem("sYG")
        sHS = mksem("sHS")
        sRT = mksem("sRT")
        sWO = mksem("sWO")
        sWB = mksem("sWB")
        sWU = mksem("sWU")
        sY3 = [mksem(f"sY3_{i}") for i in range(2)]
        sXR = [mksem(f"sXR_{i}") for i in range(2)]
        sS1 = mksem("sS1")
        sB4 = [mksem(f"sB4_{i}") for i in range(3)]
        sX2 = [mksem(f"sX2_{i}") for i in range(2)]
        sOWN = mksem("sOWN")
        sFT = [mksem(f"sFT{i}") for i in range(2)]
        sGW = mksem("sGW")
        sCR = mksem("sCR")
        sXO = mksem("sXO")
        sWU2 = [mksem(f"sWU2_{i}") for i in range(2)]
        sMX = mksem("sMX")
        sWD = mksem("sWD")
        sM2 = [mksem(f"sM2_{i}") for i in range(2)]
        sR2 = [mksem(f"sR2_{i}") for i in range(2)]
        sS2 = [mksem(f"sS2_{i}") for i in range(2)]
        sB5 = [mksem(f"sB5_{i}") for i in range(3)]
        sOUT = [mksem(f"sOUT{i}") for i in range(2)]
        sCR2 = mksem("sCR2")

        pb = [E(nc.psum_tensor(f"pb{i}", [128, 512], F32)) for i in range(8)]

        consts_bf = E(nc.sbuf_tensor("consts_bf", [128, 256], BF16))
        constf = E(nc.sbuf_tensor("constf", [128, 384], F32))
        cw_sb = E(nc.sbuf_tensor("cw_sb", [128, 192], F32))
        cb_sb = E(nc.sbuf_tensor("cb_sb", [128, 48], F32))
        cnw_sb = E(nc.sbuf_tensor("cnw_sb", [128, 48], F32))
        nw_sb = E(nc.sbuf_tensor("nw_sb", [128, 16], F32))
        onw_sb = E(nc.sbuf_tensor("onw_sb", [128, 32], F32))
        dtw_sb = E(nc.sbuf_tensor("dtw_sb", [65, 64], F32))
        nega_sb = E(nc.sbuf_tensor("nega_sb", [128, 64], F32))

        ONESB = consts_bf[:, 0:128]
        TRIUB = consts_bf[:, 128:256]
        ONESF = constf[:, 0:128]
        IDENTF = constf[:, 128:256]
        TRIUF = constf[:, 256:384]

        CPOOL_VALS = [EPS_RMS, EPS_LN, 1.0 / D, 1.0 / CONV, -0.5, -1.0, 2.0, 1.0,
                      0.5, 1.0 / HFF, -1e30, 0.0, 1.0 / DI, 3.0, 4.0, 5.0]
        cpool_sb = E(nc.sbuf_tensor("cpool_sb", [128, 16], F32))
        for _k, _v in enumerate(CPOOL_VALS):
            nc.const_aps.register(F32, float(_v), cpool_sb[:, _k:_k + 1])

        mamba_ctx = ExitStack()
        adt_sb = mamba_ctx.enter_context(nc.sbuf_tensor("adt_sb", [65, L], F32))
        hstate = mamba_ctx.enter_context(nc.sbuf_tensor("hstate", [128, DI], F32))
        alog_sb = mamba_ctx.enter_context(nc.sbuf_tensor("alog_sb", [1, 64], F32))

        b = Blk(nc)

        def _p0(b):
            def syncs(e):
                e.dma_start(consts_bf[:, 0:128], ones_bf[:, :]).then_inc(sD.h, 16)
                e.dma_start(consts_bf[:, 128:256], triu_bf[:, :]).then_inc(sD.h, 16)
                e.dma_start(constf[:, 0:128], ones_f[:, :]).then_inc(sD.h, 16)
                e.dma_start(constf[:, 128:256], ident_f[:, :]).then_inc(sD.h, 16)
                e.dma_start(constf[:, 256:384], triu_f[:, :]).then_inc(sD.h, 16)
                e.dma_start(cw_sb[:, :], cw_t[:, :]).then_inc(sD.h, 16)
                e.dma_start(cb_sb[:, :], cb_t[:, :]).then_inc(sD.h, 16)
                e.dma_start(cnw_sb[:, :], cnw_t[:, :]).then_inc(sD.h, 16)
                e.dma_start(nw_sb[:, :], nw_t[:, :]).then_inc(sD.h, 16)
                e.dma_start(onw_sb[:, :], onw_t[:, :]).then_inc(sD.h, 16)
                e.dma_start(dtw_sb[:, :], dtw_aug[:, :]).then_inc(sD.h, 16)
                e.dma_start(alog_sb[:, :], alog_r[:, :]).then_inc(sD.h, 16)
                e.dma_start(cpool_sb[:, :], cpool[:, :]).then_inc(sD.h, 16)
            b.on("sync", syncs)
            nd = sD.inc(16 * 13)

            def t_ops(e, nd=nd):
                e.wait_ge(sD.h, nd)
                e.matmul(pb[7][:, 0:64], constf[0:1, 0:128], alog_sb[0:1, 0:64],
                         start=True, stop=True).then_inc(sT.h, 1)
            b.on("tensor", t_ops)
            nt = sT.inc(1)

            def s_ops(e, nt=nt):
                e.wait_ge(sT.h, nt)
                e.activation(nega_sb[:, :], pb[7][:, 0:64], AF.Exp).then_inc(sS.h, 1)
            b.on("scalar", s_ops)
            ns = sS.inc(1)

            def v_ops(e, ns=ns):
                e.wait_ge(sS.h, ns)
                e.tensor_scalar_mul(nega_sb[:, :], nega_sb[:, :], -1.0).then_inc(sV.h, 1)
                e.memset(adt_sb[64:65, :], 1.0).then_inc(sV.h, 1)
                e.memset(hstate[:, :], 0.0).then_inc(sV.h, 1)
            b.on("vector", v_ops)
            sV.inc(3)
        _p0(b)
        b.emit()

        try:
          if BUILD_UPTO == "P0":
            raise StopBuild()
          if ONLY_PHASE in (None, "A"):
           with ExitStack() as actx:
            AE = actx.enter_context
            xw = AE(nc.sbuf_tensor("xw", [128, NDB * L], BF16))

            def xw_t(db):
                return xw[:, db * L:(db + 1) * L]

            a1ctx = ExitStack()
            A1 = a1ctx.enter_context
            xa_tm = A1(nc.sbuf_tensor("xa_tm", [128, D], F32))
            ssq_c = A1(nc.sbuf_tensor("ssq_c", [128, NTB], F32))
            rms_c = A1(nc.sbuf_tensor("rms_c", [128, NTB], F32))
            rms_tr = A1(nc.sbuf_tensor("rms_tr", [NTB, 128], F32))
            rmsrow = A1(nc.sbuf_tensor("rmsrow", [1, L], BF16))
            xa_fm = A1(nc.sbuf_tensor("xa_fm", [128, 4 * 512], F32))

            bA1 = Blk(nc)

            def _phaseA1(b):
                for tb in range(NTB):
                    def ld(e, tb=tb):
                        e.dma_start(xa_tm[:, :], x_tm[tb * 128:(tb + 1) * 128, :]).then_inc(sD.h, 16)
                    b.on("sync", ld)
                    nd = sD.inc(16)

                    def sq(e, tb=tb, nd=nd):
                        e.wait_ge(sD.h, nd)
                        e.activation(xa_tm[:, :], xa_tm[:, :], AF.Square,
                                     accum_out=ssq_c[:, tb:tb + 1]).then_inc(sS.h, 1)
                    b.on("scalar", sq)
                    ns = sS.inc(1)

                    def gate_next(e, ns=ns):
                        e.wait_ge(sS.h, ns)
                    b.on("sync", gate_next)

                def rms1(e, ns_all=sS.n):
                    e.wait_ge(sS.h, ns_all)
                    e.activation(rms_c[:, :], ssq_c[:, :], AF.Ln, scale=1.0 / D,
                                 bias=EPS_RMS).then_inc(sS.h, 1)
                    e.activation(rms_c[:, :], rms_c[:, :], AF.Exp, scale=-0.5).then_inc(sS.h, 1)
                b.on("scalar", rms1)
                ns_rms = sS.inc(2)

                def tr(e, ns_rms=ns_rms):
                    e.wait_ge(sS.h, ns_rms)
                    e.transpose(pb[7][0:NTB, 0:128], rms_c[:, :], IDENTF).then_inc(sT.h, 1)
                b.on("tensor", tr)
                nt = sT.inc(1)

                def ev(e, nt=nt):
                    e.wait_ge(sT.h, nt)
                    e.tensor_copy(rms_tr[:, :], pb[7][0:NTB, 0:128]).then_inc(sV.h, 1)
                b.on("vector", ev)
                nv = sV.inc(1)

                def st2(e, nv=nv):
                    e.wait_ge(sV.h, nv)
                    e.dma_start(bass.AP(rrow_s, 0, [[128, NTB], [1, 128]]),
                                rms_tr[:, :]).then_inc(sD.h, 16)
                b.on("sync", st2)
                nd = sD.inc(16)

                def ld_row(e, nd=nd):
                    e.wait_ge(sD.h, nd)
                    e.dma_start(rmsrow[0:1, :], rrow_s[0:1, :]).then_inc(sG.h, 16)
                b.on("gpsimd", ld_row)
                ng_row = sG.inc(16)

                last_stt = {}
                for tw in range(NTW):
                    def bc(e, tw=tw, ng_row=ng_row, nv_prev=sV.n):
                        if tw == 0:
                            e.wait_ge(sG.h, ng_row)
                        else:
                            e.wait_ge(sV.h, nv_prev)
                        e.matmul(pb[6][:, :], consts_bf[0:1, 0:128],
                                 rmsrow[0:1, tw * 512:(tw + 1) * 512],
                                 start=True, stop=True).then_inc(sT.h, 1)
                    b.on("tensor", bc)
                    nt_bc = sT.inc(1)
                    for db in range(NDB):
                        buf = db % 4
                        gidx = tw * NDB + db
                        def ldx(e, tw=tw, db=db, buf=buf, gidx=gidx):
                            if gidx >= 4:
                                e.wait_ge(sV.h, last_stt[gidx - 4])
                            e.dma_start(xa_fm[:, buf * 512:(buf + 1) * 512],
                                        x_fmT[db * 128:(db + 1) * 128,
                                              tw * 512:(tw + 1) * 512]).then_inc(sXF[buf].h, 16)
                        b.on("sync", ldx)
                        nd_x = sXF[buf].inc(16)

                        def stt(e, tw=tw, db=db, buf=buf, nd_x=nd_x, nt_bc=nt_bc):
                            e.wait_ge(sXF[buf].h, nd_x)
                            e.wait_ge(sT.h, nt_bc)
                            e.scalar_tensor_tensor(
                                xw_t(db)[:, tw * 512:(tw + 1) * 512],
                                xa_fm[:, buf * 512:(buf + 1) * 512],
                                nw_sb[:, db:db + 1], pb[6][:, :],
                                ALU.mult, ALU.mult).then_inc(sV.h, 1)
                        b.on("vector", stt)
                        last_stt[gidx] = sV.inc(1)

                t_xw = tap("xw0", [128, L], BF16)
                if t_xw is not None:
                    def dtap(e, nv=sV.n):
                        e.wait_ge(sV.h, nv)
                        e.dma_start(t_xw[:, :], xw_t(0)).then_inc(sG.h, 16)
                    b.on("gpsimd", dtap)
                    sG.inc(16)

            _phaseA1(bA1)
            bA1.emit()
            a1ctx.close()

            wa = [AE(nc.sbuf_tensor(f"wa{i}", [128, 16 * 128], BF16)) for i in range(2)]
            eva2 = [AE(nc.sbuf_tensor(f"eva{i}", [128, 8 * 512], BF16)) for i in range(2)]
            bA = Blk(nc)

            def _phaseA2(b):
                ng_w = {}

                def ldw_fb(e, fb):
                    fw = 128 if fb < NFB - 1 else FP - 128 * (NFB - 1)
                    e.dma_start(
                        bass.AP(wa[fb % 2], 0, [[16 * 128, 128], [fw, 16], [1, fw]]),
                        ipW_T.ap()[:, fb * 128:fb * 128 + fw]
                        .rearrange("(kd p) f -> p kd f", p=128)).then_inc(sWA[fb % 2].h, 16)

                def ldw0(e):
                    ldw_fb(e, 0)
                b.on("gpsimd", ldw0)
                ng_w[0] = sWA[0].inc(16)

                nvs_evac = {}
                nd_eva = {}

                def conv_row(col):
                    return col - 64 if col >= 4160 else col

                for fb in range(NFB):
                    fw = 128 if fb < NFB - 1 else FP - 128 * (NFB - 1)
                    wbuf = wa[fb % 2]
                    eva = eva2[fb % 2]
                    if fb + 1 < NFB:
                        def ldw(e, fb=fb, nt_prev=sT.n):
                            if fb >= 1:
                                e.wait_ge(sT.h, nt_prev)
                            ldw_fb(e, fb + 1)
                        b.on("gpsimd", ldw)
                        ng_w[fb + 1] = sWA[(fb + 1) % 2].inc(16)

                    col0 = fb * 128
                    segs = []
                    for lo, hi, kind in ((0, 4096, "X"), (4096, 4160, "Adt"),
                                         (4160, 5184, "B"), (5184, 6208, "C"),
                                         (6208, 10304, "gate")):
                        a0, z0 = max(col0, lo), min(col0 + fw, hi)
                        if a0 < z0:
                            segs.append((a0 - col0, z0 - col0, kind, a0))

                    nt_bank = {}
                    for tk in range(NTW):
                        def mms(e, fb=fb, fw=fw, tk=tk, wbuf=wbuf, ng=ng_w[fb]):
                            if tk == 0:
                                e.wait_ge(sWA[fb % 2].h, ng)
                            if fb >= 1:
                                pv, ps = nvs_evac[(fb - 1, tk)]
                                e.wait_ge(sV.h, pv)
                                if ps:
                                    e.wait_ge(sS.h, ps)
                            mm = None
                            for kd in range(NDB):
                                mm = e.matmul(pb[tk][0:fw, :],
                                              wbuf[:, kd * fw:(kd + 1) * fw],
                                              xw_t(kd)[:, tk * 512:(tk + 1) * 512],
                                              start=(kd == 0), stop=(kd == NDB - 1))
                            mm.then_inc(sT.h, 1)
                        b.on("tensor", mms)
                        nt_bank[tk] = sT.inc(1)

                    for tk in range(NTW):
                        v0, s0 = sV.n, sS.n
                        for (p0, p1, kind, colabs) in segs:
                            if kind == "gate":
                                def evs(e, p0=p0, p1=p1, tk=tk, nt=nt_bank[tk],
                                        eva=eva, fb=fb, ndp=nd_eva.get(fb - 2, 0)):
                                    e.wait_ge(sT.h, nt)
                                    if ndp:
                                        e.wait_ge(sEV[fb % 2].h, ndp)
                                    e.activation(eva[p0:p1, tk * 512:(tk + 1) * 512],
                                                 pb[tk][p0:p1, :], AF.Silu).then_inc(sS.h, 1)
                                b.on("scalar", evs)
                                sS.inc(1)
                            elif kind == "Adt":
                                def eva_adt(e, p0=p0, p1=p1, tk=tk, nt=nt_bank[tk],
                                            colabs=colabs):
                                    e.wait_ge(sT.h, nt)
                                    r0 = colabs - 4096
                                    e.tensor_copy(
                                        adt_sb[r0:r0 + (p1 - p0), tk * 512:(tk + 1) * 512],
                                        pb[tk][p0:p1, :]).then_inc(sV.h, 1)
                                b.on("vector", eva_adt)
                                sV.inc(1)
                            else:
                                def evv(e, p0=p0, p1=p1, tk=tk, nt=nt_bank[tk],
                                        eva=eva, fb=fb, ndp=nd_eva.get(fb - 2, 0)):
                                    e.wait_ge(sT.h, nt)
                                    if ndp:
                                        e.wait_ge(sEV[fb % 2].h, ndp)
                                    e.tensor_copy(eva[p0:p1, tk * 512:(tk + 1) * 512],
                                                  pb[tk][p0:p1, :]).then_inc(sV.h, 1)
                                b.on("vector", evv)
                                sV.inc(1)
                        nvs_evac[(fb, tk)] = (sV.n, sS.n if sS.n > s0 else 0)

                    nv_now, ns_now = sV.n, sS.n
                    for (p0, p1, kind, colabs) in segs:
                        if kind == "Adt":
                            continue
                        n = p1 - p0
                        if kind == "gate":
                            dst = gate_s[colabs - 6208: colabs - 6208 + n, :]
                        else:
                            dst = conv_s[conv_row(colabs): conv_row(colabs) + n, :]

                        def dmaout(e, p0=p0, p1=p1, dst=dst, eva=eva, fb=fb,
                                   nv_now=nv_now, ns_now=ns_now):
                            e.wait_ge(sV.h, nv_now)
                            e.wait_ge(sS.h, ns_now)
                            e.dma_start(dst, eva[p0:p1, :]).then_inc(sEV[fb % 2].h, 16)
                        b.on("sync", dmaout)
                        sEV[fb % 2].inc(16)
                    nd_eva[fb] = sEV[fb % 2].n

            if BUILD_UPTO != "A1":
                _phaseA2(bA)
                bA.emit()

            if any(t in debug_taps for t in ("proj_conv0", "gate0", "adt")):
                bT = Blk(nc)
                t1 = tap("proj_conv0", [128, L], BF16)
                t2 = tap("gate0", [128, L], BF16)
                t3 = tap("adt", [65, L], F32)
                tap_list = [(t1, conv_s[0:128, :]), (t2, gate_s[0:128, :]),
                            (t3, adt_sb[:, :])]
                n_t = sum(1 for t, _ in tap_list if t is not None)

                def _dtaps(e, tap_list=tap_list):
                    for t, src in tap_list:
                        if t is not None:
                            e.dma_start(t[:, :], src).then_inc(sG.h, 16)
                bT.on("gpsimd", _dtaps)
                sG.inc(16 * n_t)
                bT.emit()

          if ONLY_PHASE in (None, "B1"):
            if BUILD_UPTO in ("A", "A1"):
                raise StopBuild()
            with ExitStack() as b1ctx:
                BE = b1ctx.enter_context
                NWIN = 8
                WT = L // NWIN
                cin = BE(nc.sbuf_tensor("cin", [128, 48 * (WT + 4)], BF16))
                co = BE(nc.sbuf_tensor("co", [128, 48 * WT], BF16))
                cpt = BE(nc.sbuf_tensor("cpt", [128, WT], BF16))
                sqc = BE(nc.sbuf_tensor("sqc", [128, WT], BF16))
                rstd_r = BE(nc.sbuf_tensor("rstd_r", [1, WT], F32))
                rstd_rb = BE(nc.sbuf_tensor("rstd_rb", [1, WT], BF16))
                rbc = BE(nc.sbuf_tensor("rbc", [128, WT], BF16))
                cf = BE(nc.sbuf_tensor("cf", [128, 48 * WT], BF16))
                CW = WT + 4

                bB1 = Blk(nc)

                def _phaseB1(b):
                    win_done = {}
                    for w in range(NWIN):
                        t0 = w * WT
                        def ld(e, w=w, t0=t0, prev=win_done.get(w - 1),
                               ev0=sEV[0].n, ev1=sEV[1].n):
                            if w == 0:
                                e.wait_ge(sEV[0].h, ev0)
                                e.wait_ge(sEV[1].h, ev1)
                            if prev is not None:
                                e.wait_ge(sV.h, prev[0])
                                e.wait_ge(sP.h, prev[1])
                                e.wait_ge(sS.h, prev[3])
                            src0 = max(t0 - 4, 0)
                            c0 = 4 - (t0 - src0)
                            ncols = t0 + WT - src0
                            for part in range(8):
                                e.dma_start(
                                    bass.AP(cin, (part * 6) * CW + c0,
                                            [[48 * CW, 128], [CW, 6], [1, ncols]]),
                                    conv_s.ap()[part * 768:(part + 1) * 768, src0:t0 + WT]
                                    .rearrange("(cb p) t -> p cb t", p=128)
                                ).then_inc(sD.h, 16)
                        b.on("sync", ld)
                        nd = sD.inc(16 * 8)

                        if w == 0:
                            def msz(e):
                                e.memset(bass.AP(cin, 0, [[48 * CW, 128], [CW, 48], [1, 4]]),
                                         0.0).then_inc(sV.h, 1)
                            b.on("vector", msz)
                            sV.inc(1)

                        def conv_dve(e, nd=nd):
                            e.wait_ge(sD.h, nd)
                            ins = None
                            for cb in range(0, 32):
                                base = cb * CW
                                ci = lambda k: cin[:, base + k: base + k + WT]
                                ot = co[:, cb * WT:(cb + 1) * WT]
                                e.tensor_scalar(ot, ci(1), cw_sb[:, cb * 4:cb * 4 + 1],
                                                cb_sb[:, cb:cb + 1], ALU.mult, ALU.add)
                                e.scalar_tensor_tensor(ot, ci(2), cw_sb[:, cb * 4 + 1:cb * 4 + 2],
                                                       ot, ALU.mult, ALU.add)
                                e.scalar_tensor_tensor(ot, ci(3), cw_sb[:, cb * 4 + 2:cb * 4 + 3],
                                                       ot, ALU.mult, ALU.add)
                                ins = e.scalar_tensor_tensor(ot, ci(4), cw_sb[:, cb * 4 + 3:cb * 4 + 4],
                                                             ot, ALU.mult, ALU.add)
                            ins.then_inc(sV.h, 1)
                        b.on("vector", conv_dve)
                        sV.inc(1)

                        def conv_pool(e, nd=nd):
                            e.wait_ge(sD.h, nd)
                            ins = None
                            for cb in range(32, 48):
                                base = cb * CW
                                ci = lambda k: cin[:, base + k: base + k + WT]
                                ot = co[:, cb * WT:(cb + 1) * WT]
                                tmp = cpt[:, :]
                                e.tensor_scalar_mul(ot, ci(1), cw_sb[:, cb * 4:cb * 4 + 1])
                                e.tensor_scalar_add(ot, ot, cb_sb[:, cb:cb + 1])
                                for k in (2, 3, 4):
                                    e.tensor_scalar_mul(tmp, ci(k), cw_sb[:, cb * 4 + k - 1:cb * 4 + k])
                                    ins = e.tensor_add(ot, ot, tmp)
                            ins.then_inc(sP.h, 1)
                        b.on("gpsimd", conv_pool)
                        sP.inc(1)
                        nv_conv, np_conv = sV.n, sP.n

                        nt_base = sT.n
                        for cb in range(48):
                            def sq(e, cb=cb, nv_conv=nv_conv, np_conv=np_conv,
                                   nt_base=nt_base):
                                if cb == 0:
                                    e.wait_ge(sV.h, nv_conv)
                                    e.wait_ge(sP.h, np_conv)
                                else:
                                    e.wait_ge(sT.h, nt_base + cb)
                                e.activation(sqc[:, :], co[:, cb * WT:(cb + 1) * WT],
                                             AF.Square).then_inc(sS.h, 1)
                            b.on("scalar", sq)
                            ns_sq = sS.inc(1)

                            def smm(e, cb=cb, ns_sq=ns_sq, prev=win_done.get(w - 1)):
                                e.wait_ge(sS.h, ns_sq)
                                if cb == 0 and prev is not None:
                                    e.wait_ge(sS.h, prev[3])
                                e.matmul(pb[0][0:1, 0:WT], ONESB[:, 0:1], sqc[:, :],
                                         start=(cb == 0), stop=(cb == 47)).then_inc(sT.h, 1)
                            b.on("tensor", smm)
                            sT.inc(1)
                        nt_sq = sT.n

                        def rstd(e, nt_sq=nt_sq):
                            e.wait_ge(sT.h, nt_sq)
                            e.activation(rstd_r[0:1, :], pb[0][0:1, 0:WT], AF.Ln,
                                         scale=1.0 / CONV, bias=EPS_RMS)
                            e.activation(rstd_r[0:1, :], rstd_r[0:1, :], AF.Exp,
                                         scale=-0.5).then_inc(sS.h, 1)
                        b.on("scalar", rstd)
                        ns_r = sS.inc(1)

                        def rcast(e, ns_r=ns_r):
                            e.wait_ge(sS.h, ns_r)
                            e.tensor_copy(rstd_rb[0:1, :], rstd_r[0:1, :]).then_inc(sV.h, 1)
                        b.on("vector", rcast)
                        nv_rc = sV.inc(1)

                        def rbcm(e, nv_rc=nv_rc, prev=win_done.get(w - 1)):
                            e.wait_ge(sV.h, nv_rc)
                            if prev is not None:
                                e.wait_ge(sV.h, prev[4])
                            e.matmul(pb[1][:, 0:WT], consts_bf[0:1, 0:128], rstd_rb[0:1, :],
                                     start=True, stop=True).then_inc(sT.h, 1)
                        b.on("tensor", rbcm)
                        nt_rbc = sT.inc(1)

                        def rbcev(e, nt_rbc=nt_rbc):
                            e.wait_ge(sT.h, nt_rbc)
                            e.tensor_copy(rbc[:, :], pb[1][:, 0:WT]).then_inc(sV.h, 1)
                        b.on("vector", rbcev)
                        nv_rbc = sV.inc(1)

                        for cb in range(48):
                            def nrm(e, cb=cb, nv_rbc=nv_rbc, prev=win_done.get(w - 1)):
                                if cb == 0:
                                    e.wait_ge(sV.h, nv_rbc)
                                    if prev is not None:
                                        e.wait_ge(sD.h, prev[2])
                                e.scalar_tensor_tensor(
                                    cf[:, cb * WT:(cb + 1) * WT],
                                    co[:, cb * WT:(cb + 1) * WT],
                                    cnw_sb[:, cb:cb + 1], rbc[:, :],
                                    ALU.mult, ALU.mult).then_inc(sV.h, 1)
                            b.on("vector", nrm)
                            nv_n = sV.inc(1)

                            def sil(e, cb=cb, nv_n=nv_n):
                                e.wait_ge(sV.h, nv_n)
                                e.activation(cf[:, cb * WT:(cb + 1) * WT],
                                             cf[:, cb * WT:(cb + 1) * WT],
                                             AF.Silu).then_inc(sS.h, 1)
                            b.on("scalar", sil)
                            sS.inc(1)
                        ns_sil = sS.n

                        def stf(e, t0=t0, ns_sil=ns_sil):
                            e.wait_ge(sS.h, ns_sil)
                            for part in range(8):
                                e.dma_start(
                                    convf_s.ap()[part * 768:(part + 1) * 768, t0:t0 + WT]
                                    .rearrange("(cb p) t -> p cb t", p=128),
                                    bass.AP(cf, (part * 6) * WT,
                                            [[48 * WT, 128], [WT, 6], [1, WT]])
                                ).then_inc(sD.h, 16)
                        b.on("sync", stf)
                        nd_st = sD.inc(16 * 8)
                        win_done[w] = (sV.n, sP.n, nd_st, ns_r, nv_rbc)

                _phaseB1(bB1)
                bB1.emit()

                if "convf0" in debug_taps:
                    bT = Blk(nc)
                    t4 = tap("convf0", [128, L], BF16)

                    def _d(e):
                        e.dma_start(t4[:, :], convf_s[0:128, :]).then_inc(sG.h, 16)
                    bT.on("gpsimd", _d)
                    sG.inc(16)
                    bT.emit()
          if ONLY_PHASE in (None, "B2"):
            if BUILD_UPTO == "B1":
                raise StopBuild()
            with ExitStack() as b2ctx:
                B2 = b2ctx.enter_context
                cvB = B2(nc.sbuf_tensor("cvB", [128, G * CH], BF16))
                cvC = B2(nc.sbuf_tensor("cvC", [128, G * CH], BF16))
                xC = [B2(nc.sbuf_tensor(f"xC{i}", [128, DI], BF16)) for i in range(2)]
                xwd = [B2(nc.sbuf_tensor(f"xwd{i}", [128, DI], BF16)) for i in range(2)]
                btm = [B2(nc.sbuf_tensor(f"btm{i}", [128, G * 128], BF16)) for i in range(2)]
                gt = [B2(nc.sbuf_tensor(f"gt{i}", [128, DI], BF16)) for i in range(2)]
                dtp = B2(nc.sbuf_tensor("dtp", [128, 128], F32))
                acs = B2(nc.sbuf_tensor("acs", [128, 128], F32))
                negacs = B2(nc.sbuf_tensor("negacs", [128, 128], F32))
                expacs = B2(nc.sbuf_tensor("expacs", [128, 128], F32))
                dcy = B2(nc.sbuf_tensor("dcy", [128, 128], F32))
                sp1 = B2(nc.sbuf_tensor("sp1", [128, 64], F32))
                sp2 = B2(nc.sbuf_tensor("sp2", [128, 64], F32))
                posh = B2(nc.sbuf_tensor("posh", [64, 256], F32))
                hihm = B2(nc.sbuf_tensor("hihm", [64, 256], BF16))
                hif = B2(nc.sbuf_tensor("hif", [64, 256], F32))
                lohm = B2(nc.sbuf_tensor("lohm", [64, 256], BF16))
                alr = B2(nc.sbuf_tensor("alr", [1, 64], F32))
                cdrow = B2(nc.sbuf_tensor("cdrow", [1, 64], F32))
                cdbc = B2(nc.sbuf_tensor("cdbc", [128, 64], F32))
                RT2 = B2(nc.sbuf_tensor("RT2", [2, 2 * 2048], BF16))
                dscr = [B2(nc.sbuf_tensor(f"dscr{i}", [128, 384], F32)) for i in range(4)]
                ebuf = [B2(nc.sbuf_tensor(f"ebuf{i}", [128, 8 * 384], BF16)) for i in range(2)]
                cbm = [B2(nc.sbuf_tensor(f"cbm{i}", [128, 384], BF16)) for i in range(2)]
                ysc = B2(nc.sbuf_tensor("ysc", [128, 512], F32))
                hbf = B2(nc.sbuf_tensor("hbf", [128, 512], BF16))
                ytm = [B2(nc.sbuf_tensor(f"ytm{i}", [128, DI], F32)) for i in range(2)]
                ygb = B2(nc.sbuf_tensor("ygb", [128, DI], BF16))
                ygt = B2(nc.sbuf_tensor("ygt", [128, 32 * 256], BF16))
                ysq = B2(nc.sbuf_tensor("ysq", [128, 2], F32))
                yrms = B2(nc.sbuf_tensor("yrms", [128, 2], F32))

                bB2 = Blk(nc)

                def _phaseB2(b):
                    prevc = {}

                    for c in range(B2_CHUNKS if B2_CHUNKS is not None else NCH):
                        t0 = c * CH
                        pc = prevc.get(c - 1, {})

                        def ld_bc(e, t0=t0, pc=pc, c=c, nd0=sD.n):
                            if c == 0:
                                e.wait_ge(sD.h, nd0)
                            if pc:
                                e.wait_ge(sT.h, pc["cb_mm"])
                            for part, dst in ((0, cvB), (1, cvC)):
                                e.dma_start(
                                    bass.AP(dst, 0, [[G * CH, 128], [CH, G], [1, CH]]),
                                    convf_s.ap()[DI + part * 1024: DI + (part + 1) * 1024,
                                                 t0:t0 + CH]
                                    .rearrange("(g p) t -> p g t", p=128)
                                ).then_inc(sD.h, 16)
                        b.on("sync", ld_bc)
                        nd_bc = sD.inc(32)

                        def ld_xc(e, t0=t0, pc=pc, c=c, nd0=sD.n):
                            if c == 0:
                                e.wait_ge(sD.h, nd0)
                            if pc:
                                e.wait_ge(sT.h, pc["x_mm"])
                                e.wait_ge(sV.h, pc["xwd_tt"])
                            for tb in range(2):
                                for db in range(32):
                                    e.dma_start(
                                        xC[tb][:, db * 128:(db + 1) * 128],
                                        convf_s[db * 128:(db + 1) * 128,
                                                t0 + tb * 128:t0 + (tb + 1) * 128],
                                        transpose=True).then_inc(sXC.h, 16)
                        b.on("sync", ld_xc)
                        nd_xc = sXC.inc(16 * 64)

                        def ld_bt(e, t0=t0, pc=pc, c=c, nd0=sD.n):
                            if c == 0:
                                e.wait_ge(sD.h, nd0)
                            if pc:
                                e.wait_ge(sT.h, pc["st_mm"])
                            for tb in range(2):
                                for g in range(G):
                                    e.dma_start(
                                        btm[tb][:, g * 128:(g + 1) * 128],
                                        convf_s[DI + g * 128: DI + (g + 1) * 128,
                                                t0 + tb * 128:t0 + (tb + 1) * 128],
                                        transpose=True).then_inc(sBT.h, 16)
                        b.on("sync", ld_bt)
                        nd_bt = sBT.inc(16 * 16)

                        def ld_gt(e, t0=t0, pc=pc, c=c, ev0=sEV[0].n, ev1=sEV[1].n):
                            if c == 0:
                                e.wait_ge(sEV[0].h, ev0)
                                e.wait_ge(sEV[1].h, ev1)
                            if pc:
                                e.wait_ge(sV.h, pc["yg_tt"])
                            for tb in range(2):
                                for db in range(32):
                                    e.dma_start(
                                        gt[tb][:, db * 128:(db + 1) * 128],
                                        gate_s[db * 128:(db + 1) * 128,
                                               t0 + tb * 128:t0 + (tb + 1) * 128],
                                        transpose=True).then_inc(sGT.h, 16)
                        b.on("sync", ld_gt)
                        nd_gt = sGT.inc(16 * 64)

                        def dtmm(e, t0=t0, pc=pc):
                            if pc:
                                e.wait_ge(sV.h, pc["yg_tt"])
                                e.wait_ge(sS.h, pc["s_all"])
                            mm = None
                            for tb in range(2):
                                mm = e.matmul(pb[7][:, tb * 64:(tb + 1) * 64],
                                              adt_sb[0:65, t0 + tb * 128:t0 + (tb + 1) * 128],
                                              dtw_sb[0:65, :], start=True, stop=True)
                            mm.then_inc(sT.h, 1)
                        b.on("tensor", dtmm)
                        nt_dt = sT.inc(1)

                        for tb in range(2):
                            def sp_s(e, tb=tb, nt_dt=nt_dt, nv_prev=sV.n):
                                e.wait_ge(sT.h, nt_dt)
                                if tb == 1:
                                    e.wait_ge(sV.h, nv_prev)
                                e.activation(sp1[:, :], pb[7][:, tb * 64:(tb + 1) * 64], AF.Abs)
                                e.activation(sp1[:, :], sp1[:, :], AF.Exp, scale=-1.0)
                                e.activation(sp1[:, :], sp1[:, :], AF.Ln,
                                             bias=1.0).then_inc(sS.h, 1)
                            b.on("scalar", sp_s)
                            ns_sp = sS.inc(1)

                            def sp_v(e, tb=tb, nt_dt=nt_dt, ns_sp=ns_sp):
                                e.wait_ge(sT.h, nt_dt)
                                e.wait_ge(sS.h, ns_sp)
                                e.tensor_scalar_max(sp2[:, :], pb[7][:, tb * 64:(tb + 1) * 64], 0.0)
                                e.tensor_add(dtp[:, tb * 64:(tb + 1) * 64], sp1[:, :], sp2[:, :])
                                e.tensor_mul(dtp[:, tb * 64:(tb + 1) * 64],
                                             dtp[:, tb * 64:(tb + 1) * 64],
                                             nega_sb[:, :]).then_inc(sV.h, 1)
                            b.on("vector", sp_v)
                            sV.inc(1)
                        nv_A = sV.n

                        def csmm(e, nv_A=nv_A):
                            e.wait_ge(sV.h, nv_A)
                            e.matmul(pb[7][:, 128:192], TRIUF, dtp[:, 0:64],
                                     start=True, stop=True)
                            e.matmul(pb[7][:, 192:256], ONESF, dtp[:, 0:64],
                                     start=True, stop=False)
                            e.matmul(pb[7][:, 192:256], TRIUF, dtp[:, 64:128],
                                     start=False, stop=True).then_inc(sT.h, 1)
                        b.on("tensor", csmm)
                        nt_cs = sT.inc(1)

                        def acse(e, nt_cs=nt_cs):
                            e.wait_ge(sT.h, nt_cs)
                            e.tensor_copy(acs[:, :], pb[7][:, 128:256])
                            e.tensor_scalar_mul(negacs[:, :], acs[:, :], -1.0).then_inc(sV.h, 1)
                        b.on("vector", acse)
                        nv_acs = sV.inc(1)

                        def acsexp(e, nv_acs=nv_acs):
                            e.wait_ge(sV.h, nv_acs)
                            e.activation(expacs[:, :], acs[:, :], AF.Exp).then_inc(sS.h, 1)
                        b.on("scalar", acsexp)
                        ns_eacs = sS.inc(1)

                        def trmm(e, nv_acs=nv_acs):
                            e.wait_ge(sV.h, nv_acs)
                            e.transpose(pb[7][0:64, 256:384], acs[:, 0:64], IDENTF)
                            e.transpose(pb[7][0:64, 384:512], acs[:, 64:128],
                                        IDENTF).then_inc(sT.h, 1)
                        b.on("tensor", trmm)
                        nt_tr = sT.inc(1)

                        def poshe(e, nt_tr=nt_tr):
                            e.wait_ge(sT.h, nt_tr)
                            e.tensor_copy(posh[:, 0:128], pb[7][0:64, 256:384])
                            e.tensor_copy(posh[:, 128:256], pb[7][0:64, 384:512])
                            e.tensor_copy(hihm[:, :], posh[:, :])
                            e.tensor_copy(hif[:, :], hihm[:, :])
                            e.tensor_sub(hif[:, :], posh[:, :], hif[:, :])
                            e.tensor_copy(lohm[:, :], hif[:, :]).then_inc(sV.h, 1)
                        b.on("vector", poshe)
                        nv_posh = sV.inc(1)

                        def almm(e, nv_posh=nv_posh):
                            e.wait_ge(sV.h, nv_posh)
                            e.matmul(pb[7][0:1, 0:64], posh[:, 255:256],
                                     IDENTF[0:64, 0:64], start=True, stop=True).then_inc(sT.h, 1)
                        b.on("tensor", almm)
                        nt_al = sT.inc(1)

                        def alre(e, nt_al=nt_al):
                            e.wait_ge(sT.h, nt_al)
                            e.tensor_copy(alr[:, :], pb[7][0:1, 0:64]).then_inc(sV.h, 1)
                        b.on("vector", alre)
                        nv_alr = sV.inc(1)

                        def cdr(e, nv_alr=nv_alr):
                            e.wait_ge(sV.h, nv_alr)
                            e.activation(cdrow[:, :], alr[:, :], AF.Exp).then_inc(sS.h, 1)
                        b.on("scalar", cdr)
                        ns_cdr = sS.inc(1)

                        def albcmm(e, ns_cdr=ns_cdr):
                            e.wait_ge(sS.h, ns_cdr)
                            e.matmul(pb[7][:, 64:128], ONESF[0:1, 0:128], alr[0:1, :],
                                     start=True, stop=True)
                            e.matmul(pb[7][:, 128:192], ONESF[0:1, 0:128], cdrow[0:1, :],
                                     start=True, stop=True).then_inc(sT.h, 1)
                        b.on("tensor", albcmm)
                        nt_albc = sT.inc(1)

                        def dcye(e, nt_albc=nt_albc):
                            e.wait_ge(sT.h, nt_albc)
                            e.tensor_copy(cdbc[:, :], pb[7][:, 128:192])
                            e.tensor_sub(dcy[:, 0:64], pb[7][:, 64:128], acs[:, 0:64])
                            e.tensor_sub(dcy[:, 64:128], pb[7][:, 64:128],
                                         acs[:, 64:128]).then_inc(sV.h, 1)
                        b.on("vector", dcye)
                        nv_dcy = sV.inc(1)

                        def dcyexp(e, nv_dcy=nv_dcy):
                            e.wait_ge(sV.h, nv_dcy)
                            e.activation(dcy[:, :], dcy[:, :], AF.Exp).then_inc(sS.h, 1)
                        b.on("scalar", dcyexp)
                        ns_dcy = sS.inc(1)
                        nv_pb7_done = nv_dcy

                        def xwdf(e, ns_dcy=ns_dcy, nd_xc=nd_xc, pc=pc):
                            e.wait_ge(sS.h, ns_dcy)
                            e.wait_ge(sXC.h, nd_xc)
                            if pc:
                                e.wait_ge(sT.h, pc["st_mm"])
                            for tb in range(2):
                                xv = bass.AP(xC[tb], 0, [[DI, 128], [64, 64], [1, 64]])
                                dv = bass.AP(dcy, tb * 64, [[128, 128], [1, 64], [0, 64]])
                                ov = bass.AP(xwd[tb], 0, [[DI, 128], [64, 64], [1, 64]])
                                e.tensor_tensor(ov, xv, dv, ALU.mult).then_inc(sV.h, 1)
                        b.on("vector", xwdf)
                        nv_xwd = sV.inc(2)

                        def hst(e, nv_posh=nv_posh, pc=pc):
                            if pc:
                                e.wait_ge(sRT.h, pc["rt_ld"])
                            e.wait_ge(sV.h, nv_posh)
                            e.dma_start(bass.AP(hilo_s, 0, [[NH * CH, 1], [CH, 64], [1, CH]]),
                                        hihm[:, :]).then_inc(sHS.h, 16)
                            e.dma_start(bass.AP(hilo_s, NH * CH, [[NH * CH, 1], [CH, 64], [1, CH]]),
                                        lohm[:, :]).then_inc(sHS.h, 16)
                        b.on("sync", hst)
                        nd_hs = sHS.inc(32)

                        rt_ld_n = {}
                        rb_done = {}

                        if B2_STAGE is not None and B2_STAGE <= 1:
                            continue
                        cur = {}
                        pair_gate = {}
                        nt_rb_last = 0
                        nt_cb_last = 0
                        for g in range(G):
                            gs = g % 2

                            def rtld(e, g=g, gs=gs, nd_hs=nd_hs, pc=pc,
                                     rbg=rb_done.get(g - 2, 0)):
                                if g >= 2:
                                    e.wait_ge(sT.h, rbg)
                                elif pc:
                                    e.wait_ge(sT.h, pc["rb_mm"])
                                e.wait_ge(sHS.h, nd_hs)
                                e.dma_start(RT2[0:2, gs * 2048:(gs + 1) * 2048],
                                            hilo_s[0:2, g * 2048:(g + 1) * 2048]
                                            ).then_inc(sRT.h, 16)
                            b.on("sync", rtld)
                            rt_ld_n[g] = sRT.inc(16)

                            def cbmm(e, g=g, gs=gs, nd_bc=nd_bc, pc=pc,
                                     nv_cbm_prev=(cur.get(("cbm", g - 2)) or 0)):
                                if g == 0:
                                    e.wait_ge(sD.h, nd_bc)
                                if g < 2 and pc:
                                    e.wait_ge(sV.h, pc["yg_tt"])
                                if nv_cbm_prev:
                                    e.wait_ge(sV.h, nv_cbm_prev)
                                e.matmul(pb[2 + gs][:, 0:256],
                                         cvB[:, g * CH:g * CH + 128],
                                         cvC[:, g * CH:(g + 1) * CH], start=True, stop=True)
                                e.matmul(pb[2 + gs][:, 256:384],
                                         cvB[:, g * CH + 128:(g + 1) * CH],
                                         cvC[:, g * CH + 128:(g + 1) * CH],
                                         start=True, stop=True).then_inc(sT.h, 1)
                            b.on("tensor", cbmm)
                            nt_cb = sT.inc(1)
                            nt_cb_last = nt_cb

                            def cbme(e, g=g, gs=gs, nt_cb=nt_cb,
                                     prev_lc=(cur.get(("lc", g - 2)) or 0)):
                                e.wait_ge(sT.h, nt_cb)
                                if prev_lc:
                                    e.wait_ge(sV.h, prev_lc)
                                e.tensor_tensor(cbm[gs][:, 0:128], pb[2 + gs][:, 0:128],
                                                TRIUB, ALU.mult)
                                e.tensor_copy(cbm[gs][:, 128:256], pb[2 + gs][:, 128:256])
                                e.tensor_tensor(cbm[gs][:, 256:384], pb[2 + gs][:, 256:384],
                                                TRIUB, ALU.mult).then_inc(sV.h, 1)
                            b.on("vector", cbme)
                            nv_cbm = sV.inc(1)
                            cur[("cbm", g)] = nv_cbm

                            for hh in range(0, 8, 2):
                                h0 = g * 8 + hh
                                pslot = (h0 // 2) % 2

                                def rbmm(e, h0=h0, g=g, gs=gs, pslot=pslot, pc=pc,
                                         nd_rt=rt_ld_n[g], gate=pair_gate.get(h0 - 4)):
                                    if h0 % 8 == 0:
                                        e.wait_ge(sRT.h, nd_rt)
                                    if h0 < 4 and pc:
                                        e.wait_ge(sV.h, pc["yg_tt"])
                                    if gate:
                                        e.wait_ge(sV.h, gate)
                                    e.matmul(pb[pslot][:, :], consts_bf[0:2, 0:128],
                                             RT2[0:2, gs * 2048 + (h0 % 8) * CH:
                                                 gs * 2048 + (h0 % 8 + 2) * CH],
                                             start=True, stop=True).then_inc(sT.h, 1)
                                b.on("tensor", rbmm)
                                nt_rb = sT.inc(1)
                                nt_rb_last = nt_rb
                                rb_done[g] = nt_rb

                                for hi in range(2):
                                    h = h0 + hi
                                    hl = h % 64
                                    hoff = hi * CH

                                    def ets(e, h=h, hl=hl, hoff=hoff, pslot=pslot,
                                            dslot=(h % 4), nt_rb=nt_rb, pc=pc,
                                            eg=(cur.get(("exp", h - 4)) or 0)):
                                        e.wait_ge(sT.h, nt_rb)
                                        if h < 4 and pc:
                                            e.wait_ge(sS.h, pc["s_all"])
                                        if eg:
                                            e.wait_ge(sS.h, eg)
                                        ds = dscr[dslot]
                                        e.tensor_scalar(ds[:, 0:128],
                                                        pb[pslot][:, hoff:hoff + 128],
                                                        negacs[:, hl:hl + 1], 0.0,
                                                        ALU.add, ALU.min)
                                        e.tensor_scalar(ds[:, 128:256],
                                                        pb[pslot][:, hoff + 128:hoff + 256],
                                                        negacs[:, hl:hl + 1], None,
                                                        ALU.add)
                                        e.tensor_scalar(ds[:, 256:384],
                                                        pb[pslot][:, hoff + 128:hoff + 256],
                                                        negacs[:, 64 + hl:64 + hl + 1],
                                                        0.0, ALU.add, ALU.min).then_inc(sV.h, 1)
                                    b.on("vector", ets)
                                    nv_ets = sV.inc(1)
                                    if hi == 1:
                                        pair_gate[h0] = nv_ets

                                    def eexp(e, h=h, g=g, dslot=(h % 4), nv_ets=nv_ets,
                                             pc=pc, eb_gate=(cur.get(("ymm", g - 2)) or 0)):
                                        e.wait_ge(sV.h, nv_ets)
                                        if g < 2 and h % 8 == 0 and pc:
                                            e.wait_ge(sT.h, pc["x_mm"])
                                        if eb_gate and h % 8 == 0:
                                            e.wait_ge(sT.h, eb_gate)
                                        e.activation(
                                            ebuf[g % 2][:, (h % 8) * 384:(h % 8 + 1) * 384],
                                            dscr[dslot][:, :], AF.Exp).then_inc(sS.h, 1)
                                    b.on("scalar", eexp)
                                    cur[("exp", h)] = sS.inc(1)

                                    def lcf(e, h=h, g=g, nv_cbm=nv_cbm,
                                            ns_exp=cur[("exp", h)]):
                                        e.wait_ge(sS.h, ns_exp)
                                        e.wait_ge(sV.h, nv_cbm)
                                        sl = ebuf[g % 2][:, (h % 8) * 384:(h % 8 + 1) * 384]
                                        e.tensor_tensor(sl, sl, cbm[g % 2][:, :],
                                                        ALU.mult).then_inc(sV.h, 1)
                                    b.on("vector", lcf)
                                    cur[("lc_h", h)] = sV.inc(1)
                            nv_lc_g = sV.n
                            cur[("lc", g)] = nv_lc_g

                            if B2_STAGE is not None and B2_STAGE <= 2:
                                continue
                            def stmm(e, g=g, nd_bt=nd_bt, nv_xwd=nv_xwd,
                                     hupd_prev=(cur.get(("hupd", g - 1)) or 0),
                                     hupd_pc=(pc.get("hupd_all") if pc else 0)):
                                if g == 0:
                                    e.wait_ge(sBT.h, nd_bt)
                                    e.wait_ge(sV.h, nv_xwd)
                                    if hupd_pc:
                                        e.wait_ge(sV.h, hupd_pc)
                                if hupd_prev:
                                    e.wait_ge(sV.h, hupd_prev)
                                mm = None
                                for hh in range(8):
                                    h = g * 8 + hh
                                    for tb in range(2):
                                        mm = e.matmul(pb[6][:, hh * 64:(hh + 1) * 64],
                                                      btm[tb][:, g * 128:(g + 1) * 128],
                                                      xwd[tb][:, h * 64:(h + 1) * 64],
                                                      start=(tb == 0), stop=(tb == 1))
                                mm.then_inc(sT.h, 1)
                            b.on("tensor", stmm)
                            nt_st = sT.inc(1)
                            cur[("st", g)] = nt_st

                            def hbfc(e, g=g, pc=pc,
                                     yprev=(cur.get(("ymm", g - 1)) or 0)):
                                if pc:
                                    e.wait_ge(sV.h, pc["hupd_all"])
                                    if g == 0:
                                        e.wait_ge(sT.h, pc["x_mm"])
                                if yprev:
                                    e.wait_ge(sT.h, yprev)
                                e.tensor_copy(hbf[:, :],
                                              hstate[:, g * 512:(g + 1) * 512]).then_inc(sV.h, 1)
                            b.on("vector", hbfc)
                            nv_hbf = sV.inc(1)

                            for tb in range(2):
                                def ycmm(e, g=g, tb=tb, nv_lc_g=nv_lc_g, nd_xc=nd_xc,
                                         nv_hbf=nv_hbf, pc=pc,
                                         yev_prev=(cur.get(("yev", (g, tb - 1))) or
                                                   cur.get(("yev", (g - 1, 1))) or 0)):
                                    e.wait_ge(sV.h, max(nv_lc_g, nv_hbf))
                                    if g == 0 and tb == 0:
                                        e.wait_ge(sXC.h, nd_xc)
                                        if pc:
                                            e.wait_ge(sV.h, pc["yg_tt"])
                                    if yev_prev:
                                        e.wait_ge(sV.h, yev_prev)
                                    es = ebuf[g % 2]
                                    for hh in range(8):
                                        h = g * 8 + hh
                                        sl = es[:, hh * 384:(hh + 1) * 384]
                                        if tb == 0:
                                            e.matmul(pb[4][:, hh * 64:(hh + 1) * 64],
                                                     sl[:, 0:128],
                                                     xC[0][:, h * 64:(h + 1) * 64],
                                                     start=True, stop=True)
                                        else:
                                            e.matmul(pb[4][:, hh * 64:(hh + 1) * 64],
                                                     sl[:, 128:256],
                                                     xC[0][:, h * 64:(h + 1) * 64],
                                                     start=True, stop=False)
                                            e.matmul(pb[4][:, hh * 64:(hh + 1) * 64],
                                                     sl[:, 256:384],
                                                     xC[1][:, h * 64:(h + 1) * 64],
                                                     start=False, stop=True)
                                        mm = e.matmul(pb[5][:, hh * 64:(hh + 1) * 64],
                                                      cvC[:, g * CH + tb * 128:g * CH + (tb + 1) * 128],
                                                      hbf[:, hh * 64:(hh + 1) * 64],
                                                      start=True, stop=True)
                                    mm.then_inc(sT.h, 1)
                                b.on("tensor", ycmm)
                                nt_y = sT.inc(1)
                                cur[("ymm", g)] = nt_y

                                def yev(e, g=g, tb=tb, nt_y=nt_y, ns_eacs=ns_eacs):
                                    e.wait_ge(sT.h, nt_y)
                                    e.wait_ge(sS.h, ns_eacs)
                                    ev = bass.AP(expacs, tb * 64 + g * 8,
                                                 [[128, 128], [1, 8], [0, 64]])
                                    yv = bass.AP(pb[5], 0, [[512, 128], [64, 8], [1, 64]])
                                    yo = bass.AP(ysc, 0, [[512, 128], [64, 8], [1, 64]])
                                    e.tensor_tensor(yo, yv, ev, ALU.mult)
                                    e.tensor_tensor(ytm[tb][:, g * 512:(g + 1) * 512],
                                                    ysc[:, :], pb[4][:, :],
                                                    ALU.add).then_inc(sV.h, 1)
                                b.on("vector", yev)
                                cur[("yev", (g, tb))] = sV.inc(1)

                            def hupd(e, g=g, nt_st=cur[("st", g)],
                                     nt_ylast=cur[("ymm", g)], nv_dcy=nv_dcy):
                                e.wait_ge(sT.h, nt_st)
                                e.wait_ge(sT.h, nt_ylast)
                                hv = hstate[:, g * 512:(g + 1) * 512]
                                hv3 = bass.AP(hstate, g * 512, [[DI, 128], [64, 8], [0, 64]])
                                cv = bass.AP(cdbc, g * 8, [[64, 128], [1, 8], [0, 64]])
                                hv3o = bass.AP(hstate, g * 512, [[DI, 128], [64, 8], [1, 64]])
                                e.tensor_tensor(hv3o, hv3o, cv, ALU.mult)
                                e.tensor_tensor(hv, hv, pb[6][:, :],
                                                ALU.add).then_inc(sV.h, 1)
                            b.on("vector", hupd)
                            cur[("hupd", g)] = sV.inc(1)
                        cur_hupd_all = sV.n
                        if B2_STAGE is not None and B2_STAGE <= 3:
                            continue

                        for tb in range(2):
                            def ysqf(e, tb=tb, last_yev=cur[("yev", (G - 1, 1))],
                                     nd_gt=nd_gt, pc=pc):
                                e.wait_ge(sV.h, last_yev)
                                if tb == 0 and pc:
                                    e.wait_ge(sYT.h, pc["ygt_tr"])
                                e.activation(ygb[:, :], ytm[tb][:, :], AF.Square,
                                             accum_out=ysq[:, tb:tb + 1]).then_inc(sS.h, 1)
                            b.on("scalar", ysqf)
                            sS.inc(1)
                        ns_ysq = sS.n

                        def yrmsf(e, ns_ysq=ns_ysq):
                            e.wait_ge(sS.h, ns_ysq)
                            e.activation(yrms[:, :], ysq[:, :], AF.Ln, scale=1.0 / DI,
                                         bias=EPS_RMS)
                            e.activation(yrms[:, :], yrms[:, :], AF.Exp,
                                         scale=-0.5).then_inc(sS.h, 1)
                        b.on("scalar", yrmsf)
                        ns_yrms = sS.inc(1)

                        if B2_STAGE is not None and B2_STAGE <= 4:
                            continue
                        nv_yg = 0
                        nd_ygt = 0
                        for tb in range(2):
                            def ygf(e, tb=tb, ns_yrms=ns_yrms, nd_gt=nd_gt,
                                    pc=pc, nd_prev_tr=nd_ygt):
                                e.wait_ge(sS.h, ns_yrms)
                                e.wait_ge(sGT.h, nd_gt)
                                if tb == 1:
                                    e.wait_ge(sYT.h, nd_prev_tr)
                                e.tensor_scalar_mul(ygb[:, :], ytm[tb][:, :],
                                                    yrms[:, tb:tb + 1])
                                e.tensor_mul(ygb[:, :], ygb[:, :],
                                             gt[tb][:, :]).then_inc(sV.h, 1)
                            b.on("vector", ygf)
                            nv_yg = sV.inc(1)

                            if B2_STAGE is not None and B2_STAGE <= 5:
                                continue
                            def ygtr(e, tb=tb, nv_yg=nv_yg, pc=pc):
                                e.wait_ge(sV.h, nv_yg)
                                if tb == 0 and pc:
                                    e.wait_ge(sYG.h, pc["ygt_st"])
                                for db in range(32):
                                    e.dma_start(ygt[:, db * 256 + tb * 128:db * 256 + (tb + 1) * 128],
                                                ygb[:, db * 128:(db + 1) * 128],
                                                transpose=True).then_inc(sYT.h, 16)
                            b.on("sync", ygtr)
                            nd_ygt = sYT.inc(16 * 32)

                        if B2_STAGE is not None and B2_STAGE <= 6:
                            continue
                        def ygst(e, t0=t0, nd_ygt=nd_ygt):
                            e.wait_ge(sYT.h, nd_ygt)
                            for db in range(32):
                                e.dma_start(ygT_s[db * 128:(db + 1) * 128, t0:t0 + CH],
                                            ygt[:, db * 256:(db + 1) * 256]).then_inc(sYG.h, 16)
                        b.on("sync", ygst)
                        nd_ygs = sYG.inc(16 * 32)

                        prevc[c] = {
                            "cb_mm": nt_cb_last,
                            "x_mm": cur[("ymm", G - 1)],
                            "xwd_tt": nv_xwd,
                            "st_mm": cur[("st", G - 1)],
                            "yg_tt": nv_yg,
                            "rb_mm": nt_rb_last,
                            "rt_ld": rt_ld_n[G - 1],
                            "pb7_done": nv_xwd,
                            "hupd_all": cur_hupd_all,
                            "ygt_tr": nd_ygt,
                            "ygt_st": nd_ygs,
                            "s_all": sS.n,
                        }

                _phaseB2(bB2)
                bB2.emit()

                if "ygt0" in debug_taps:
                    bT = Blk(nc)
                    t5 = tap("ygt0", [128, L], BF16)

                    def _d5(e):
                        e.dma_start(t5[:, :], ygT_s[0:128, :]).then_inc(sG.h, 16)
                    bT.on("gpsimd", _d5)
                    sG.inc(16)
                    bT.emit()

          mamba_ctx.close()
          if ONLY_PHASE in (None, "B3"):
            if BUILD_UPTO == "B2":
                raise StopBuild()
            with ExitStack() as b3ctx:
                B3 = b3ctx.enter_context
                wo_f = [B3(nc.sbuf_tensor(f"wo_f{i}", [128, D], F32)) for i in range(2)]
                wo_b = [B3(nc.sbuf_tensor(f"wo_b{i}", [128, D], BF16)) for i in range(2)]
                wub = B3(nc.sbuf_tensor("wub", [128, 32 * 512], BF16))
                ytl = [B3(nc.sbuf_tensor(f"ytl{i}", [128, 32 * 128], BF16)) for i in range(2)]
                xr3 = [B3(nc.sbuf_tensor(f"xr3{i}", [128, 512], F32)) for i in range(2)]
                s1t = [B3(nc.sbuf_tensor(f"s1t{i}", [128, 512], F32)) for i in range(2)]

                bB3 = Blk(nc)

                def _phaseB3(b):
                    sc_done = {}
                    st_done = {}
                    for ib in range(32):
                        sl = ib % 2
                        def ldo(e, ib=ib, sl=sl, gate=st_done.get(ib - 2, 0)):
                            if gate:
                                e.wait_ge(sWB.h, gate)
                            e.dma_start(wo_f[sl][:, :],
                                        outW_T[ib * 128:(ib + 1) * 128, :]).then_inc(sWO.h, 16)
                        b.on("sync", ldo)
                        ndo = sWO.inc(16)

                        def sco(e, ib=ib, sl=sl, ndo=ndo, gate=st_done.get(ib - 2, 0)):
                            e.wait_ge(sWO.h, ndo)
                            if gate:
                                e.wait_ge(sWB.h, gate)
                            e.activation(wo_b[sl][:, :], wo_f[sl][:, :], AF.Copy,
                                         scale=onw_sb[:, ib:ib + 1]).then_inc(sS.h, 1)
                        b.on("scalar", sco)
                        nso = sS.inc(1)
                        sc_done[ib] = nso

                        def sto(e, ib=ib, sl=sl, nso=nso):
                            e.wait_ge(sS.h, nso)
                            e.dma_start(woutb_s[ib * 128:(ib + 1) * 128, :],
                                        wo_b[sl][:, :]).then_inc(sWB.h, 16)
                        b.on("sync", sto)
                        st_done[ib] = sWB.inc(16)
                    nd_wall = sWB.n

                    ev_done = {}
                    mm_done = {}
                    ld_done = {}
                    st1_done = {}
                    for dmt in range(4):
                        def ldwu(e, dmt=dmt, nd_wall=nd_wall,
                                 gate=mm_done.get((dmt - 1, 31), 0)):
                            if dmt == 0:
                                e.wait_ge(sWB.h, nd_wall)
                            if gate:
                                e.wait_ge(sT.h, gate)
                            e.dma_start(
                                bass.AP(wub, 0, [[32 * 512, 128], [512, 32], [1, 512]]),
                                woutb_s.ap()[:, dmt * 512:(dmt + 1) * 512]
                                .rearrange("(kd p) dm -> p kd dm", p=128)).then_inc(sWU.h, 16)
                        b.on("sync", ldwu)
                        nd_wu = sWU.inc(16)

                        for tb in range(32):
                            sl = tb % 2
                            def ldy(e, dmt=dmt, tb=tb, sl=sl, ndyg=sYG.n,
                                    gate=mm_done.get((dmt, tb - 2), 0) or
                                         (mm_done.get((dmt - 1, 30 + tb), 0) if tb < 2 else 0)):
                                if dmt == 0 and tb < 2:
                                    e.wait_ge(sYG.h, ndyg)
                                if gate:
                                    e.wait_ge(sT.h, gate)
                                e.dma_start(
                                    bass.AP(ytl[sl], 0, [[32 * 128, 128], [128, 32], [1, 128]]),
                                    ygT_s.ap()[:, tb * 128:(tb + 1) * 128]
                                    .rearrange("(kd p) t -> p kd t", p=128)
                                ).then_inc(sY3[sl].h, 16)
                            b.on("sync", ldy)
                            nd_y = sY3[sl].inc(16)

                            def ldxr(e, dmt=dmt, tb=tb, sl=sl,
                                     gate=ev_done.get((dmt, tb - 2), 0) or
                                          (ev_done.get((dmt - 1, 30 + tb), 0) if tb < 2 else 0)):
                                if gate:
                                    e.wait_ge(sV.h, gate)
                                e.dma_start(xr3[sl][:, :],
                                            x_tm[tb * 128:(tb + 1) * 128,
                                                 dmt * 512:(dmt + 1) * 512]).then_inc(sXR[sl].h, 16)
                            b.on("sync", ldxr)
                            nd_xr = sXR[sl].inc(16)

                            def mm3(e, dmt=dmt, tb=tb, sl=sl, nd_y=nd_y, nd_wu=nd_wu,
                                    evg=ev_done.get((dmt, tb - 2), 0) or
                                        (ev_done.get((dmt - 1, 30 + tb), 0) if tb < 2 else 0)):
                                e.wait_ge(sY3[sl].h, nd_y)
                                if tb == 0:
                                    e.wait_ge(sWU.h, nd_wu)
                                if evg:
                                    e.wait_ge(sV.h, evg)
                                mm = None
                                for kd in range(32):
                                    mm = e.matmul(pb[sl][:, :],
                                                  ytl[sl][:, kd * 128:(kd + 1) * 128],
                                                  wub[:, kd * 512:(kd + 1) * 512],
                                                  start=(kd == 0), stop=(kd == 31))
                                mm.then_inc(sT.h, 1)
                            b.on("tensor", mm3)
                            mm_done[(dmt, tb)] = sT.inc(1)

                            def ev3(e, dmt=dmt, tb=tb, sl=sl, nt=mm_done[(dmt, tb)],
                                    nd_xr=nd_xr, stg=st1_done.get((dmt, tb - 2), 0) or
                                    (st1_done.get((dmt - 1, 30 + tb), 0) if tb < 2 else 0)):
                                e.wait_ge(sT.h, nt)
                                e.wait_ge(sXR[sl].h, nd_xr)
                                if stg:
                                    e.wait_ge(sS1.h, stg)
                                e.scalar_tensor_tensor(
                                    s1t[sl][:, :], xr3[sl][:, :], 2.0,
                                    pb[sl][:, :], ALU.mult,
                                    ALU.add).then_inc(sV.h, 1)
                            b.on("vector", ev3)
                            ev_done[(dmt, tb)] = sV.inc(1)

                            def st1(e, dmt=dmt, tb=tb, sl=sl, nv=ev_done[(dmt, tb)]):
                                e.wait_ge(sV.h, nv)
                                e.dma_start(s1_s[tb * 128:(tb + 1) * 128,
                                                 dmt * 512:(dmt + 1) * 512],
                                            s1t[sl][:, :]).then_inc(sS1.h, 16)
                            b.on("sync", st1)
                            st1_done[(dmt, tb)] = sS1.inc(16)

                _phaseB3(bB3)
                bB3.emit()

                if "s1r" in debug_taps:
                    bT = Blk(nc)
                    t8 = tap("s1r", [128, D])

                    def _d8(e, t8=t8):
                        e.dma_start(t8[:, :], s1_s[0:128, :]).then_inc(sG.h, 16)
                    bT.on("gpsimd", _d8)
                    sG.inc(16)
                    bT.emit()

          if ONLY_PHASE in (None, "B4"):
            if BUILD_UPTO == "B3":
                raise StopBuild()
            with ExitStack() as b4ctx:
                B4 = b4ctx.enter_context
                s4 = [B4(nc.sbuf_tensor(f"s4_{i}", [128, D], F32)) for i in range(3)]
                scr4 = B4(nc.sbuf_tensor("scr4", [128, D], BF16))
                scol = B4(nc.sbuf_tensor("scol", [128, 3], F32))
                vcol = B4(nc.sbuf_tensor("vcol", [128, 3], F32))
                mcol = B4(nc.sbuf_tensor("mcol", [128, 3], F32))
                negm = B4(nc.sbuf_tensor("negm", [128, 3], F32))
                rstd = B4(nc.sbuf_tensor("rstd", [128, 3], F32))
                vln = B4(nc.sbuf_tensor("vln", [128, 1], F32))
                x2t = [B4(nc.sbuf_tensor(f"x2t{i}", [128, D], F32)) for i in range(2)]
                x2bt = [B4(nc.sbuf_tensor(f"x2bt{i}", [128, D], BF16)) for i in range(2)]
                oix = B4(nc.sbuf_tensor("oix", [1, 4], mybir.dt.int32))

                bB4 = Blk(nc)

                def _phaseB4(b):
                    st_done = {}
                    cm_done = {}
                    fin_n = {}

                    def _emit_x2(b, tf):
                        slf = tf % 3
                        sl2 = tf % 2

                        def x2f(e, tf=tf, slf=slf, sl2=sl2, nfin=fin_n[tf],
                                stg=st_done.get(tf - 2, 0)):
                            e.wait_ge(sS.h, nfin)
                            if stg:
                                e.wait_ge(sX2[sl2].h, stg)
                            e.tensor_scalar(x2t[sl2][:, :], s4[slf][:, :],
                                            mcol[:, slf:slf + 1],
                                            rstd[:, slf:slf + 1],
                                            ALU.subtract, ALU.mult)
                            e.tensor_copy(x2bt[sl2][:, :],
                                          x2t[sl2][:, :]).then_inc(sV.h, 1)
                        b.on("vector", x2f)
                        cm_done[tf] = sV.inc(1)

                        def st4(e, tf=tf, sl2=sl2, nv=cm_done[tf]):
                            e.wait_ge(sV.h, nv)
                            e.dma_start(x2_s[tf * 128:(tf + 1) * 128, :],
                                        x2t[sl2][:, :]).then_inc(sX2[sl2].h, 16)
                            e.dma_start(x2b_s[tf * 128:(tf + 1) * 128, :],
                                        x2bt[sl2][:, :]).then_inc(sX2[sl2].h, 16)
                        b.on("sync", st4)
                        st_done[tf] = sX2[sl2].inc(32)

                    for tb in range(NTB):
                        sl = tb % 3

                        def ld4(e, tb=tb, sl=sl, nds1=sS1.n, gate=cm_done.get(tb - 3, 0)):
                            if tb < 3:
                                e.wait_ge(sS1.h, nds1)
                            if gate:
                                e.wait_ge(sV.h, gate)
                            e.dma_start(s4[sl][:, :],
                                        s1_s[tb * 128:(tb + 1) * 128, :]).then_inc(sB4[sl].h, 16)
                        b.on("sync", ld4)
                        nd4 = sB4[sl].inc(16)

                        def sc_ops(e, tb=tb, sl=sl, nd4=nd4, gate=cm_done.get(tb - 3, 0)):
                            e.wait_ge(sB4[sl].h, nd4)
                            if gate:
                                e.wait_ge(sV.h, gate)
                            e.activation(scr4[:, :], s4[sl][:, :], AF.Copy,
                                         accum_out=scol[:, sl:sl + 1])
                            if tb >= 1:
                                sp = (tb - 1) % 3
                                e.activation(scr4[:, :], s4[sp][:, :], AF.Square,
                                             bias=negm[:, sp:sp + 1],
                                             accum_out=vcol[:, sp:sp + 1])
                            else:
                                e.activation(scr4[:, :], s4[sl][:, :], AF.Square,
                                             accum_out=vcol[:, 2:3])
                            e.activation(mcol[:, sl:sl + 1], scol[:, sl:sl + 1],
                                         AF.Copy, scale=1.0 / D)
                            e.activation(negm[:, sl:sl + 1], mcol[:, sl:sl + 1],
                                         AF.Copy, scale=-1.0)
                            if tb >= 2:
                                sf = (tb - 2) % 3
                                e.activation(vln[:, :], vcol[:, sf:sf + 1], AF.Ln,
                                             scale=1.0 / D, bias=EPS_LN)
                                e.activation(rstd[:, sf:sf + 1], vln[:, :], AF.Exp,
                                             scale=-0.5).then_inc(sS.h, 1)
                        b.on("scalar", sc_ops)
                        if tb >= 2:
                            fin_n[tb - 2] = sS.inc(1)
                            _emit_x2(b, tb - 2)

                    N1 = NTB - 1
                    N2 = NTB - 2

                    def sc_tail(e, gate=cm_done.get(NTB - 4, 0)):
                        e.wait_ge(sV.h, gate)
                        e.activation(scr4[:, :], s4[N1 % 3][:, :], AF.Square,
                                     bias=negm[:, N1 % 3:N1 % 3 + 1],
                                     accum_out=vcol[:, N1 % 3:N1 % 3 + 1])
                        sf = N2 % 3
                        e.activation(vln[:, :], vcol[:, sf:sf + 1], AF.Ln,
                                     scale=1.0 / D, bias=EPS_LN)
                        e.activation(rstd[:, sf:sf + 1], vln[:, :], AF.Exp,
                                     scale=-0.5).then_inc(sS.h, 1)
                        e.activation(scr4[:, :], s4[N1 % 3][:, :], AF.Copy)
                        sf = N1 % 3
                        e.activation(vln[:, :], vcol[:, sf:sf + 1], AF.Ln,
                                     scale=1.0 / D, bias=EPS_LN)
                        e.activation(rstd[:, sf:sf + 1], vln[:, :], AF.Exp,
                                     scale=-0.5).then_inc(sS.h, 1)
                    b.on("scalar", sc_tail)
                    fin_n[N2] = sS.inc(1)
                    _emit_x2(b, N2)
                    fin_n[N1] = sS.inc(1)
                    _emit_x2(b, N1)
                    nd_x2all = (sX2[0].n, sX2[1].n)

                    def ldoix(e):
                        e.dma_start(oix[:, :], ownoff[:, :]).then_inc(sG.h, 16)
                    b.on("gpsimd", ldoix)
                    ng_oix = sG.inc(16)

                    def own(e, nd_x2all=nd_x2all, ng_oix=ng_oix):
                        e.wait_ge(sX2[0].h, nd_x2all[0])
                        e.wait_ge(sX2[1].h, nd_x2all[1])
                        e.wait_ge(sG.h, ng_oix)
                        with e.register("ro") as ro, e.register("rp") as rp:
                            e.reg_load(ro, oix[0:1, 0:1])
                            for part in range(4):
                                e.reg_add(rp, ro, part * 512 * D)
                                e.dma_start(
                                    x2own_s[part * 512:(part + 1) * 512, :],
                                    bass.AP(x2b_s, rp, [[D, 512], [1, D]])
                                ).then_inc(sOWN.h, 16)
                                e.dma_start(
                                    x2owf_s[part * 512:(part + 1) * 512, :],
                                    bass.AP(x2_s, rp, [[D, 512], [1, D]])
                                ).then_inc(sOWN.h, 16)
                    b.on("gpsimd", own)
                    sOWN.inc(16 * 8)

                _phaseB4(bB4)
                bB4.emit()

                if any(t in debug_taps for t in ("x2r", "x2ownr")):
                    bT = Blk(nc)
                    t6 = tap("x2r", [128, D])
                    t7 = tap("x2ownr", [128, D], BF16)
                    n_t = sum(1 for t in (t6, t7) if t is not None)

                    def _d6(e, t6=t6, t7=t7, nx2=(sX2[0].n, sX2[1].n),
                            nown=sOWN.n):
                        e.wait_ge(sX2[0].h, nx2[0])
                        e.wait_ge(sX2[1].h, nx2[1])
                        e.wait_ge(sOWN.h, nown)
                        if t6 is not None:
                            e.dma_start(t6[:, :], x2_s[0:128, :]).then_inc(sG.h, 16)
                        if t7 is not None:
                            e.dma_start(t7[:, :], x2own_s[0:128, :]).then_inc(sG.h, 16)
                    bT.on("gpsimd", _d6)
                    sG.inc(16 * n_t)
                    bT.emit()

          if ONLY_PHASE in (None, "F"):
            if BUILD_UPTO == "B4":
                raise StopBuild()
            with ExitStack() as fctx:
                FF = fctx.enter_context
                crows1 = FF(nc.sbuf_tensor("crows1", [1, NE * TOWN], BF16))
                upA32b = FF(nc.sbuf_tensor("upA32b", [128, 16 * 256], BF16))
                gateA32b = FF(nc.sbuf_tensor("gateA32b", [128, 16 * 256], BF16))
                gwb = FF(nc.sbuf_tensor("gwb", [128, 128], BF16))
                da2 = [[FF(nc.sbuf_tensor(f"da2_{t}_{w}", [128, 1024], BF16))
                        for w in range(2)] for t in range(2)]
                oixF = FF(nc.sbuf_tensor("oixF", [1, 4], mybir.dt.int32))

                with ExitStack() as factx:
                    FA = factx.enter_context
                    xft = [FA(nc.sbuf_tensor(f"xft{i}", [128, D], BF16))
                           for i in range(2)]
                    glb = FA(nc.sbuf_tensor("glb", [128, 256], F32))
                    m1b = FA(nc.sbuf_tensor("m1b", [128, 32], F32))
                    m2b = FA(nc.sbuf_tensor("m2b", [128, 32], F32))
                    mask1b = FA(nc.sbuf_tensor("mask1b", [128, 256], F32))
                    mask2b = FA(nc.sbuf_tensor("mask2b", [128, 256], F32))
                    gl2b = FA(nc.sbuf_tensor("gl2b", [128, 256], F32))
                    exp1b = FA(nc.sbuf_tensor("exp1b", [128, 32], F32))
                    exp2b = FA(nc.sbuf_tensor("exp2b", [128, 32], F32))
                    r12 = FA(nc.sbuf_tensor("r12", [128, 2], F32))
                    sum2 = FA(nc.sbuf_tensor("sum2", [1, 2], F32))
                    sum128 = FA(nc.sbuf_tensor("sum128", [128, 2], F32))
                    lnv = FA(nc.sbuf_tensor("lnv", [128, 2], F32))
                    inv128 = FA(nc.sbuf_tensor("inv128", [128, 2], F32))
                    wf1b = FA(nc.sbuf_tensor("wf1b", [128, 32], F32))
                    wf2b = FA(nc.sbuf_tensor("wf2b", [128, 32], F32))
                    cb1 = FA(nc.sbuf_tensor("cb1", [128, 256], F32))
                    cball = FA(nc.sbuf_tensor("cball", [128, 256], F32))
                    fjunk = FA(nc.sbuf_tensor("fjunk", [128, D], F32))
                    crow_sb = FA(nc.sbuf_tensor("crow_sb", [8, L], F32))

                    bFA = Blk(nc)

                    def _phaseFA(b):
                        def ldgw(e, nx2=(sX2[0].n, sX2[1].n), nown=sOWN.n):
                            if nx2[0]:
                                e.wait_ge(sX2[0].h, nx2[0])
                            if nx2[1]:
                                e.wait_ge(sX2[1].h, nx2[1])
                            if nown:
                                e.wait_ge(sOWN.h, nown)
                            for kd in range(16):
                                e.dma_start(
                                    gwb[:, kd * 8:(kd + 1) * 8],
                                    gwT[kd * 128:(kd + 1) * 128, :]).then_inc(sGW.h, 16)
                            for kd in range(16):
                                e.dma_start(
                                    upA32b[:, kd * 256:(kd + 1) * 256],
                                    upA_pk[kd * 128:(kd + 1) * 128, :]).then_inc(sGW.h, 16)
                            for kd in range(16):
                                e.dma_start(
                                    gateA32b[:, kd * 256:(kd + 1) * 256],
                                    gateA_pk[kd * 128:(kd + 1) * 128, :]).then_inc(sGW.h, 16)
                        b.on("gpsimd", ldgw)
                        ng_all = sGW.inc(16 * 48)

                        glmm_done = {}
                        glev_done = {}
                        for tb in range(NTB):
                            sl = tb % 2

                            def fttr(e, tb=tb, sl=sl, nx2=(sX2[0].n, sX2[1].n),
                                     gate=glmm_done.get(tb - 2, 0)):
                                if tb < 2 and nx2[0]:
                                    e.wait_ge(sX2[0].h, nx2[0])
                                if tb < 2 and nx2[1]:
                                    e.wait_ge(sX2[1].h, nx2[1])
                                if gate:
                                    e.wait_ge(sT.h, gate)
                                for kd in range(16):
                                    e.dma_start(
                                        xft[sl][:, kd * 128:(kd + 1) * 128],
                                        x2b_s[tb * 128:(tb + 1) * 128,
                                              kd * 128:(kd + 1) * 128],
                                        transpose=True).then_inc(sFT[sl].h, 16)
                            b.on("sync", fttr)
                            nd_ft = sFT[sl].inc(16 * 16)

                            def glmm(e, tb=tb, sl=sl, nd_ft=nd_ft,
                                     ngw=ng_all, evg=glev_done.get(tb - 2, 0)):
                                e.wait_ge(sFT[sl].h, nd_ft)
                                if tb == 0:
                                    e.wait_ge(sGW.h, ngw)
                                if evg:
                                    e.wait_ge(sS.h, evg)
                                for kd in range(16):
                                    e.matmul(pb[6 + tb % 2][:, 0:8],
                                             xft[sl][:, kd * 128:(kd + 1) * 128],
                                             gwb[:, kd * 8:(kd + 1) * 8],
                                             start=(kd == 0),
                                             stop=(kd == 15)).then_inc(sT.h, 1)
                            b.on("tensor", glmm)
                            glmm_done[tb] = sT.inc(16)

                            def glev(e, tb=tb, ntm=glmm_done[tb]):
                                e.wait_ge(sT.h, ntm)
                                e.activation(glb[:, tb * 8:(tb + 1) * 8],
                                             pb[6 + tb % 2][:, 0:8],
                                             AF.Copy).then_inc(sS.h, 1)
                            b.on("scalar", glev)
                            glev_done[tb] = sS.inc(1)

                        ns_glall = sS.n

                        glb3 = bass.AP(glb, 0, [[256, 128], [8, 32], [1, 8]])
                        gl2b3 = bass.AP(gl2b, 0, [[256, 128], [8, 32], [1, 8]])
                        mask1b3 = bass.AP(mask1b, 0, [[256, 128], [8, 32], [1, 8]])
                        mask2b3 = bass.AP(mask2b, 0, [[256, 128], [8, 32], [1, 8]])
                        m1bc = bass.AP(m1b, 0, [[32, 128], [1, 32], [0, 8]])
                        m2bc = bass.AP(m2b, 0, [[32, 128], [1, 32], [0, 8]])
                        wf1bc = bass.AP(wf1b, 0, [[32, 128], [1, 32], [0, 8]])
                        wf2bc = bass.AP(wf2b, 0, [[32, 128], [1, 32], [0, 8]])
                        cb13 = bass.AP(cb1, 0, [[256, 128], [8, 32], [1, 8]])
                        cball3 = bass.AP(cball, 0, [[256, 128], [8, 32], [1, 8]])

                        def top2(e, ns=ns_glall):
                            e.wait_ge(sS.h, ns)
                            e.reduce_max(m1b[:, :], glb3, AX.X)
                            e.memset(fjunk[:, :], 0.0)
                            e.tensor_tensor(mask1b3, glb3, m1bc, ALU.is_ge)
                            e.scalar_tensor_tensor(gl2b[:, :], mask1b[:, :],
                                                   -1e30, glb[:, :],
                                                   ALU.mult, ALU.add)
                            e.reduce_max(m2b[:, :], gl2b3, AX.X)
                            e.memset(fjunk[:, :], 0.0)
                            e.tensor_tensor(mask2b3, gl2b3, m2bc,
                                            ALU.is_ge).then_inc(sV.h, 1)
                        b.on("vector", top2)
                        nv_masks = sV.inc(1)

                        def exps(e, nv=nv_masks):
                            e.wait_ge(sV.h, nv)
                            e.activation(exp1b[:, :], m1b[:, :], AF.Exp,
                                         accum_out=r12[:, 0:1])
                            e.activation(exp2b[:, :], m2b[:, :], AF.Exp,
                                         accum_out=r12[:, 1:2])
                            e.activation(fjunk[:, :], fjunk[:, :],
                                         AF.Copy).then_inc(sS.h, 1)
                        b.on("scalar", exps)
                        ns_exp = sS.inc(1)

                        def summ(e, ns=ns_exp):
                            e.wait_ge(sS.h, ns)
                            e.matmul(pb[5][0:1, 0:2], ONESF[:, 0:1], r12[:, :],
                                     start=True, stop=True).then_inc(sT.h, 1)
                        b.on("tensor", summ)
                        nt_sum = sT.inc(1)

                        def sume(e, nt=nt_sum):
                            e.wait_ge(sT.h, nt)
                            e.activation(sum2[0:1, :], pb[5][0:1, 0:2], AF.Copy)
                            e.activation(fjunk[:, :], fjunk[:, :],
                                         AF.Copy).then_inc(sS.h, 1)
                        b.on("scalar", sume)
                        ns_s2 = sS.inc(1)

                        def sumbc(e, ns=ns_s2):
                            e.wait_ge(sS.h, ns)
                            e.matmul(pb[4][:, 0:2], ONESF[0:1, :], sum2[0:1, :],
                                     start=True, stop=True).then_inc(sT.h, 1)
                        b.on("tensor", sumbc)
                        nt_bc = sT.inc(1)

                        def wfs(e, nt=nt_bc):
                            e.wait_ge(sT.h, nt)
                            e.activation(sum128[:, :], pb[4][:, 0:2], AF.Copy)
                            e.activation(lnv[:, :], sum128[:, :], AF.Ln)
                            e.activation(inv128[:, :], lnv[:, :], AF.Exp,
                                         scale=-1.0)
                            e.activation(fjunk[:, :], fjunk[:, :], AF.Copy)
                            e.activation(wf1b[:, :], exp1b[:, :], AF.Copy,
                                         scale=inv128[:, 0:1])
                            e.activation(wf2b[:, :], exp2b[:, :], AF.Copy,
                                         scale=inv128[:, 1:2]).then_inc(sS.h, 1)
                        b.on("scalar", wfs)
                        ns_wf = sS.inc(1)

                        def cmk(e, ns=ns_wf):
                            e.wait_ge(sS.h, ns)
                            e.tensor_tensor(cb13, mask1b3, wf1bc, ALU.mult)
                            e.tensor_tensor(cball3, mask2b3, wf2bc, ALU.mult)
                            e.tensor_tensor(cball[:, :], cball[:, :], cb1[:, :],
                                            ALU.add)
                            e.memset(fjunk[:, :], 0.0).then_inc(sV.h, 1)
                        b.on("vector", cmk)
                        nv_cb = sV.inc(1)

                        ctr_done = {}
                        cte_done = {}
                        for tb in range(NTB):
                            def ctr(e, tb=tb, nv=nv_cb,
                                    evg=cte_done.get(tb - 2, 0)):
                                if tb == 0:
                                    e.wait_ge(sV.h, nv)
                                if evg:
                                    e.wait_ge(sS.h, evg)
                                e.matmul(pb[6 + tb % 2][0:8, 0:128],
                                         cball[:, tb * 8:(tb + 1) * 8],
                                         IDENTF[:, :],
                                         start=True, stop=True).then_inc(sT.h, 1)
                            b.on("tensor", ctr)
                            ctr_done[tb] = sT.inc(1)

                            def cte(e, tb=tb, nt=ctr_done[tb]):
                                e.wait_ge(sT.h, nt)
                                e.activation(crow_sb[0:8, tb * 128:(tb + 1) * 128],
                                             pb[6 + tb % 2][0:8, 0:128],
                                             AF.Copy).then_inc(sS.h, 1)
                            b.on("scalar", cte)
                            cte_done[tb] = sS.inc(1)

                        def crst(e, ns=sS.n):
                            e.wait_ge(sS.h, ns)
                            e.dma_start(crow_s[:, :], crow_sb[:, :]).then_inc(sCR.h, 16)
                        b.on("sync", crst)
                        nd_crst = sCR.inc(16)

                        def crld(e, nd=nd_crst):
                            e.wait_ge(sCR.h, nd)
                            e.dma_start(oixF[:, :], ownoff[:, :]).then_inc(sCR2.h, 16)
                            e.wait_ge(sCR2.h, 16)
                            with e.register("cro") as ro, e.register("crp") as rp:
                                e.reg_load(ro, oixF[0:1, 1:2])
                                for ex in range(NE):
                                    e.reg_add(rp, ro, ex * L)
                                    e.dma_start(
                                        crows1[0:1, ex * TOWN:(ex + 1) * TOWN],
                                        bass.AP(crow_s, rp, [[L, 1], [1, TOWN]])
                                    ).then_inc(sCR2.h, 16)
                        b.on("gpsimd", crld)
                        sCR2.inc(16 * 9)

                        if "glr" in debug_taps:
                            t8 = tap("glr", [128, 256])
                            t9 = tap("cbr", [128, 256])

                            def _d8(e, t8=t8, t9=t9, nv=nv_cb):
                                e.wait_ge(sV.h, nv)
                                e.dma_start(t8[:, :], glb[:, :]).then_inc(sG.h, 16)
                                e.dma_start(t9[:, :], cball[:, :]).then_inc(sG.h, 16)
                            b.on("gpsimd", _d8)
                            sG.inc(32)

                    _phaseFA(bFA)
                    bFA.emit()

                fstop = (BUILD_UPTO == "FA")

                NTBO = TOWN // 128
                prev_th = {"pe": 0, "sc": 0, "vp": 0, "pp": 0, "mx": 0}
                for th in ([] if fstop else range(2)):
                  with ExitStack() as fbctx:
                    FB = fbctx.enter_context
                    xfo = FB(nc.sbuf_tensor(f"xfo{th}", [128, 16 * 1024], BF16))
                    cbc = FB(nc.sbuf_tensor(f"cbc{th}", [128, NE * 1024], BF16))
                    ua32 = FB(nc.sbuf_tensor(f"ua32_{th}", [128, 2 * 1024], BF16))
                    ga32 = FB(nc.sbuf_tensor(f"ga32_{th}", [128, 2 * 1024], BF16))
                    bu = FB(nc.sbuf_tensor(f"bu{th}", [128, 1024], BF16))
                    bg = FB(nc.sbuf_tensor(f"bg{th}", [128, 1024], BF16))
                    wupb = [FB(nc.sbuf_tensor(f"wupb{th}_{i}", [128, 16 * 128], BF16))
                            for i in range(2)]
                    wgb = [FB(nc.sbuf_tensor(f"wgb{th}_{i}", [128, 16 * 128], BF16))
                           for i in range(2)]
                    upBb = [FB(nc.sbuf_tensor(f"upBb{th}_{i}", [128, 256], BF16))
                            for i in range(2)]
                    gateBb = [FB(nc.sbuf_tensor(f"gateBb{th}_{i}", [128, 256], BF16))
                              for i in range(2)]
                    dAb = [FB(nc.sbuf_tensor(f"dAb{th}_{i}", [128, 256], BF16))
                           for i in range(2)]
                    aub = [FB(nc.sbuf_tensor(f"aub{th}_{i}", [128, 2048], BF16))
                           for i in range(2)]
                    seb = [FB(nc.sbuf_tensor(f"seb{th}_{i}", [128, 2048], BF16))
                           for i in range(2)]
                    agb = [FB(nc.sbuf_tensor(f"agb{th}_{i}", [128, 2048], BF16))
                           for i in range(2)]
                    hbw = FB(nc.sbuf_tensor(f"hbw{th}", [128, 2 * 2048], BF16))
                    mxb = FB(nc.sbuf_tensor(f"mxb{th}", [128, 1024], F32))
                    da_sb = [FB(nc.sbuf_tensor(f"da_sb{th}_{w}", [128, 1024], F32))
                             for w in range(2)]
                    ddt = [FB(nc.sbuf_tensor(f"ddt{th}_{i}", [128, 512], F32))
                           for i in range(2)]

                    bFB = Blk(nc)

                    def _phaseFB(b, th=th):
                        def xotr(e, th=th, nown=sOWN.n, pe_prev=prev_th["pe"]):
                            if th == 0 and nown:
                                e.wait_ge(sOWN.h, nown)
                            if pe_prev:
                                e.wait_ge(sT.h, pe_prev)
                            for otb in range(8):
                                for kd in range(16):
                                    e.dma_start(
                                        xfo[:, kd * 1024 + otb * 128:
                                            kd * 1024 + (otb + 1) * 128],
                                        x2own_s[(th * 8 + otb) * 128:
                                                (th * 8 + otb + 1) * 128,
                                                kd * 128:(kd + 1) * 128],
                                        transpose=True).then_inc(sXO.h, 16)
                        b.on("sync", xotr)
                        nd_xo = sXO.inc(16 * 128)

                        uga_ev = {}
                        for pj, (apan, dst) in enumerate(((upA32b, ua32),
                                                          (gateA32b, ga32))):
                            def ugam(e, pj=pj, apan=apan, nd_xo=nd_xo,
                                     ngw=sGW.n, sc_prev=sS.n, vp_prev=sV.n,
                                     evp=uga_ev.get(pj - 1, 0)):
                                e.wait_ge(sXO.h, nd_xo)
                                e.wait_ge(sGW.h, ngw)
                                if sc_prev:
                                    e.wait_ge(sS.h, sc_prev)
                                if vp_prev:
                                    e.wait_ge(sV.h, vp_prev)
                                if evp:
                                    e.wait_ge(sS.h, evp)
                                for w in range(2):
                                    for kd in range(16):
                                        e.matmul(pb[pj * 4 + w * 2 + 0][:, :],
                                                 apan[:, kd * 256 + w * 128:
                                                      kd * 256 + (w + 1) * 128],
                                                 xfo[:, kd * 1024:kd * 1024 + 512],
                                                 start=(kd == 0), stop=(kd == 15))
                                        e.matmul(pb[pj * 4 + w * 2 + 1][:, :],
                                                 apan[:, kd * 256 + w * 128:
                                                      kd * 256 + (w + 1) * 128],
                                                 xfo[:, kd * 1024 + 512:
                                                     kd * 1024 + 1024],
                                                 start=(kd == 0),
                                                 stop=(kd == 15)).then_inc(sT.h, 1)
                            b.on("tensor", ugam)
                            nt_uga = sT.inc(32)

                            def ugae(e, pj=pj, dst=dst, nt=nt_uga):
                                e.wait_ge(sT.h, nt)
                                for w in range(2):
                                    for tt in range(2):
                                        e.activation(
                                            dst[:, w * 1024 + tt * 512:
                                                w * 1024 + (tt + 1) * 512],
                                            pb[pj * 4 + w * 2 + tt][:, :],
                                            AF.Copy, scale=2.0).then_inc(sS.h, 1)
                            b.on("scalar", ugae)
                            uga_ev[pj] = sS.inc(4)

                        cbc_mm = {}
                        cbc_ev = {}
                        for ci in range(16):
                            ex, tt = divmod(ci, 2)

                            def cbmm(e, ci=ci, ex=ex, tt=tt, ncr=sCR2.n,
                                     uev=uga_ev[1], evp=cbc_ev.get(ci - 2, 0)):
                                if ci == 0:
                                    e.wait_ge(sCR2.h, ncr)
                                    e.wait_ge(sS.h, uev)
                                if evp:
                                    e.wait_ge(sS.h, evp)
                                e.matmul(
                                    pb[6 + ci % 2][:, :],
                                    ONESB[0:1, :],
                                    crows1[0:1, ex * TOWN + th * 1024 + tt * 512:
                                           ex * TOWN + th * 1024 + (tt + 1) * 512],
                                    start=True, stop=True).then_inc(sT.h, 1)
                            b.on("tensor", cbmm)
                            cbc_mm[ci] = sT.inc(1)

                            def cbev(e, ci=ci, ex=ex, tt=tt, nt=cbc_mm[ci]):
                                e.wait_ge(sT.h, nt)
                                e.activation(
                                    cbc[:, ex * 1024 + tt * 512:
                                        ex * 1024 + (tt + 1) * 512],
                                    pb[6 + ci % 2][:, :],
                                    AF.Copy).then_inc(sS.h, 1)
                            b.on("scalar", cbev)
                            cbc_ev[ci] = sS.inc(1)

                        UNITS = ((0, 0), (0, 1), (1, 0), (1, 1))

                        def ukey(fb, ui):
                            if ui < 0:
                                fb, ui = fb - 1, ui + 4
                            if fb < 0:
                                return None
                            return (fb,) + UNITS[ui]

                        fbdone = {}
                        base_ev = {}
                        lu_mm = {}
                        lg_mm = {}
                        au_mg = {}
                        se_ok = {}
                        ag_mg = {}
                        h_ok = {}
                        d_mm = {}
                        d_ev = {}
                        da_ok = {}
                        mxst = {}
                        for fb in range(44):
                            sl = fb % 2

                            def ldfb(e, fb=fb, sl=sl, gate=fbdone.get(fb - 2, 0),
                                     vg=prev_th["vp"] if fb < 2 else 0):
                                if gate:
                                    e.wait_ge(sT.h, gate)
                                if vg:
                                    e.wait_ge(sV.h, vg)
                                e.dma_start(
                                    bass.AP(wupb[sl], 0,
                                            [[2048, 128], [128, 16], [1, 128]]),
                                    bass.AP(upW_T, fb * 128,
                                            [[HFF, 128], [128 * HFF, 16], [1, 128]])
                                ).then_inc(sWU2[sl].h, 16)
                                e.dma_start(
                                    bass.AP(wgb[sl], 0,
                                            [[2048, 128], [128, 16], [1, 128]]),
                                    bass.AP(gateW_T, fb * 128,
                                            [[HFF, 128], [128 * HFF, 16], [1, 128]])
                                ).then_inc(sWU2[sl].h, 16)
                                for w in range(2):
                                    e.dma_start(
                                        upBb[sl][:, w * 128:(w + 1) * 128],
                                        upB_stk[w * 128:(w + 1) * 128,
                                                fb * 128:(fb + 1) * 128]
                                    ).then_inc(sWU2[sl].h, 16)
                                    e.dma_start(
                                        gateBb[sl][:, w * 128:(w + 1) * 128],
                                        gateB_stk[w * 128:(w + 1) * 128,
                                                  fb * 128:(fb + 1) * 128]
                                    ).then_inc(sWU2[sl].h, 16)
                                e.dma_start(
                                    dAb[sl][:, :],
                                    dA_pk[fb * 128:(fb + 1) * 128, :]
                                ).then_inc(sWU2[sl].h, 16)
                            b.on("gpsimd", ldfb)
                            nd_fb = sWU2[sl].inc(16 * 7)

                            def basem(e, fb=fb, sl=sl, nd=nd_fb,
                                      bev=base_ev.get(fb - 1, 0),
                                      lmg=au_mg.get((fb - 1, 1, 1), 0),
                                      gmg=ag_mg.get((fb - 1, 1, 1), 0),
                                      dev=d_ev.get((fb - 1, 1), 0),
                                      cev=cbc_ev[15]):
                                e.wait_ge(sWU2[fb % 2].h, nd)
                                if fb == 0:
                                    e.wait_ge(sS.h, cev)
                                if bev:
                                    e.wait_ge(sS.h, bev)
                                if lmg:
                                    e.wait_ge(sV.h, lmg)
                                if gmg:
                                    e.wait_ge(sV.h, gmg)
                                if dev:
                                    e.wait_ge(sS.h, dev)
                                for kd in range(16):
                                    e.matmul(pb[0][:, :],
                                             wupb[sl][:, kd * 128:(kd + 1) * 128],
                                             xfo[:, kd * 1024:kd * 1024 + 512],
                                             start=(kd == 0), stop=(kd == 15))
                                    e.matmul(pb[1][:, :],
                                             wupb[sl][:, kd * 128:(kd + 1) * 128],
                                             xfo[:, kd * 1024 + 512:kd * 1024 + 1024],
                                             start=(kd == 0), stop=(kd == 15))
                                    e.matmul(pb[2][:, :],
                                             wgb[sl][:, kd * 128:(kd + 1) * 128],
                                             xfo[:, kd * 1024:kd * 1024 + 512],
                                             start=(kd == 0), stop=(kd == 15))
                                    e.matmul(pb[3][:, :],
                                             wgb[sl][:, kd * 128:(kd + 1) * 128],
                                             xfo[:, kd * 1024 + 512:kd * 1024 + 1024],
                                             start=(kd == 0),
                                             stop=(kd == 15)).then_inc(sT.h, 1)
                            b.on("tensor", basem)
                            nt_base = sT.inc(16)

                            def basee(e, fb=fb, nt=nt_base):
                                e.wait_ge(sT.h, nt)
                                e.activation(bu[:, 0:512], pb[0][:, :], AF.Copy)
                                e.activation(bu[:, 512:1024], pb[1][:, :], AF.Copy)
                                e.activation(bg[:, 0:512], pb[2][:, :], AF.Copy)
                                e.activation(bg[:, 512:1024], pb[3][:, :],
                                             AF.Copy).then_inc(sS.h, 1)
                            b.on("scalar", basee)
                            base_ev[fb] = sS.inc(1)

                            for ui, (w, tt) in enumerate(((0, 0), (0, 1),
                                                          (1, 0), (1, 1))):
                                slu = ui % 2

                                def lumm(e, fb=fb, sl=sl, w=w, tt=tt,
                                         bev=base_ev[fb],
                                         dprev=d_ev.get((fb, w - 1), 0) or
                                               d_ev.get((fb - 1, 1), 0),
                                         mgp=au_mg.get((fb, w, tt - 1), 0) or
                                             au_mg.get((fb, w - 1, 1), 0) or
                                             au_mg.get((fb - 1, 1, 1), 0)):
                                    e.wait_ge(sS.h, bev)
                                    if dprev:
                                        e.wait_ge(sS.h, dprev)
                                    if mgp:
                                        e.wait_ge(sV.h, mgp)
                                    for j in range(4):
                                        e.matmul(
                                            pb[j][:, :],
                                            upBb[sl][32 * j:32 * j + 32,
                                                     w * 128:(w + 1) * 128],
                                            ua32[32 * j:32 * j + 32,
                                                 w * 1024 + tt * 512:
                                                 w * 1024 + (tt + 1) * 512],
                                            start=True, stop=True,
                                            tile_position=(32 * j, 0)
                                        ).then_inc(sT.h, 1)
                                b.on("tensor", lumm)
                                lu_mm[(fb, w, tt)] = sT.inc(4)

                                def aumrg(e, fb=fb, tt=tt, slu=slu,
                                          nt=lu_mm[(fb, w, tt)],
                                          pg=h_ok.get(ukey(fb, ui - 2), 0) or
                                             (prev_th["pp"] if fb == 0 else 0)):
                                    e.wait_ge(sT.h, nt)
                                    if pg:
                                        e.wait_ge(sP.h, pg)
                                    for j in range(4):
                                        e.tensor_tensor(
                                            aub[slu][:, j * 512:(j + 1) * 512],
                                            pb[j][:, :],
                                            bu[:, tt * 512:(tt + 1) * 512],
                                            ALU.add).then_inc(sV.h, 1)
                                b.on("vector", aumrg)
                                au_mg[(fb, w, tt)] = sV.inc(4)

                                def silu(e, slu=slu, nv=au_mg[(fb, w, tt)],
                                         pg=h_ok.get(ukey(fb, ui - 2), 0) or
                                            (prev_th["pp"] if fb == 0 else 0)):
                                    e.wait_ge(sV.h, nv)
                                    if pg:
                                        e.wait_ge(sP.h, pg)
                                    for j in range(4):
                                        e.activation(
                                            seb[slu][:, j * 512:(j + 1) * 512],
                                            aub[slu][:, j * 512:(j + 1) * 512],
                                            AF.Silu).then_inc(sS.h, 1)
                                b.on("scalar", silu)
                                se_ok[(fb, w, tt)] = sS.inc(4)

                                def lgmm(e, fb=fb, sl=sl, w=w, tt=tt,
                                         bev=base_ev[fb],
                                         mgp=ag_mg.get((fb, w, tt - 1), 0) or
                                             ag_mg.get((fb, w - 1, 1), 0) or
                                             ag_mg.get((fb - 1, 1, 1), 0)):
                                    e.wait_ge(sS.h, bev)
                                    if mgp:
                                        e.wait_ge(sV.h, mgp)
                                    for j in range(4):
                                        e.matmul(
                                            pb[4 + j][:, :],
                                            gateBb[sl][32 * j:32 * j + 32,
                                                       w * 128:(w + 1) * 128],
                                            ga32[32 * j:32 * j + 32,
                                                 w * 1024 + tt * 512:
                                                 w * 1024 + (tt + 1) * 512],
                                            start=True, stop=True,
                                            tile_position=(32 * j, 0)
                                        ).then_inc(sT.h, 1)
                                b.on("tensor", lgmm)
                                lg_mm[(fb, w, tt)] = sT.inc(4)

                                def agmrg(e, fb=fb, tt=tt, slu=slu,
                                          nt=lg_mm[(fb, w, tt)],
                                          pg=h_ok.get(ukey(fb, ui - 2), 0) or
                                             (prev_th["pp"] if fb == 0 else 0)):
                                    e.wait_ge(sT.h, nt)
                                    if pg:
                                        e.wait_ge(sP.h, pg)
                                    for j in range(4):
                                        e.tensor_tensor(
                                            agb[slu][:, j * 512:(j + 1) * 512],
                                            pb[4 + j][:, :],
                                            bg[:, tt * 512:(tt + 1) * 512],
                                            ALU.add).then_inc(sV.h, 1)
                                b.on("vector", agmrg)
                                ag_mg[(fb, w, tt)] = sV.inc(4)

                                def poolh(e, fb=fb, w=w, tt=tt, slu=slu,
                                          nse=se_ok[(fb, w, tt)],
                                          nag=ag_mg[(fb, w, tt)],
                                          ndm=d_mm.get((fb, w - 1), 0) or
                                              d_mm.get((fb - 1, 1), 0),
                                          mxp=mxst.get((fb - 1, 1), 0) or
                                              (prev_th["mx"] if fb == 0 else 0)):
                                    e.wait_ge(sS.h, nse)
                                    e.wait_ge(sV.h, nag)
                                    if ndm:
                                        e.wait_ge(sT.h, ndm)
                                    if w == 0 and mxp:
                                        e.wait_ge(sMX.h, mxp)
                                    for j in range(4):
                                        e.tensor_tensor(
                                            aub[slu][:, j * 512:(j + 1) * 512],
                                            seb[slu][:, j * 512:(j + 1) * 512],
                                            cbc[:, (4 * w + j) * 1024 + tt * 512:
                                                (4 * w + j) * 1024 + (tt + 1) * 512],
                                            ALU.mult)
                                        e.tensor_tensor(
                                            hbw[:, tt * 2048 + j * 512:
                                                tt * 2048 + (j + 1) * 512],
                                            aub[slu][:, j * 512:(j + 1) * 512],
                                            agb[slu][:, j * 512:(j + 1) * 512],
                                            ALU.mult)
                                    for j in range(4):
                                        if w == 0 and j == 0:
                                            ins = e.tensor_copy(
                                                mxb[:, tt * 512:(tt + 1) * 512],
                                                hbw[:, tt * 2048:tt * 2048 + 512])
                                        else:
                                            ins = e.tensor_tensor(
                                                mxb[:, tt * 512:(tt + 1) * 512],
                                                mxb[:, tt * 512:(tt + 1) * 512],
                                                hbw[:, tt * 2048 + j * 512:
                                                    tt * 2048 + (j + 1) * 512],
                                                ALU.add)
                                        if j == 3:
                                            ins.then_inc(sP.h, 1)
                                b.on("gpsimd", poolh)
                                h_ok[(fb, w, tt)] = sP.inc(1)

                                if tt == 1:
                                    def dmms(e, fb=fb, sl=sl, w=w,
                                             nh0=h_ok[(fb, w, 0)],
                                             nh1=h_ok[(fb, w, 1)],
                                             dep=d_ev.get((fb, w - 1), 0) or
                                                 d_ev.get((fb - 1, 1), 0)):
                                        e.wait_ge(sP.h, nh0)
                                        e.wait_ge(sP.h, nh1)
                                        if dep:
                                            e.wait_ge(sS.h, dep)
                                        for j in range(4):
                                            for t2 in range(2):
                                                e.matmul(
                                                    pb[t2][32 * j:32 * j + 32, :],
                                                    dAb[sl][:, w * 128 + 32 * j:
                                                            w * 128 + 32 * j + 32],
                                                    hbw[:, t2 * 2048 + j * 512:
                                                        t2 * 2048 + (j + 1) * 512],
                                                    start=True, stop=True,
                                                    tile_position=(0, 32 * j)
                                                ).then_inc(sT.h, 1)
                                    b.on("tensor", dmms)
                                    d_mm[(fb, w)] = sT.inc(8)

                                    def deva(e, fb=fb, w=w, nt=d_mm[(fb, w)],
                                             dap=da_ok.get((fb, w - 1), 0) or
                                                 da_ok.get((fb - 1, 1), 0) or
                                                 (prev_th["pp"] if fb == 0
                                                  else 0)):
                                        e.wait_ge(sT.h, nt)
                                        if dap:
                                            e.wait_ge(sP.h, dap)
                                        e.activation(ddt[0][:, :], pb[0][:, :],
                                                     AF.Copy)
                                        e.activation(ddt[1][:, :], pb[1][:, :],
                                                     AF.Copy).then_inc(sS.h, 1)
                                    b.on("scalar", deva)
                                    d_ev[(fb, w)] = sS.inc(1)

                                    def dadd(e, fb=fb, w=w, nev=d_ev[(fb, w)]):
                                        e.wait_ge(sS.h, nev)
                                        for t2 in range(2):
                                            if fb == 0:
                                                ins = e.tensor_copy(
                                                    da_sb[w][:, t2 * 512:
                                                             (t2 + 1) * 512],
                                                    ddt[t2][:, :])
                                            else:
                                                ins = e.tensor_tensor(
                                                    da_sb[w][:, t2 * 512:
                                                             (t2 + 1) * 512],
                                                    da_sb[w][:, t2 * 512:
                                                             (t2 + 1) * 512],
                                                    ddt[t2][:, :], ALU.add)
                                            if t2 == 1:
                                                ins.then_inc(sP.h, 1)
                                    b.on("gpsimd", dadd)
                                    da_ok[(fb, w)] = sP.inc(1)

                            fbdone[fb] = d_mm[(fb, 1)]

                            for tt in range(2):
                                def mxs(e, fb=fb, th=th, tt=tt,
                                        nh=h_ok[(fb, 1, tt)]):
                                    e.wait_ge(sP.h, nh)
                                    e.dma_start(
                                        mixed_s[fb * 128:(fb + 1) * 128,
                                                th * 1024 + tt * 512:
                                                th * 1024 + (tt + 1) * 512],
                                        mxb[:, tt * 512:(tt + 1) * 512]
                                    ).then_inc(sMX.h, 16)
                                b.on("gpsimd", mxs)
                                mxst[(fb, tt)] = sMX.inc(16)

                        def dacast(e, th=th, n0=da_ok[(43, 0)], n1=da_ok[(43, 1)]):
                            e.wait_ge(sP.h, n0)
                            e.wait_ge(sP.h, n1)
                            e.activation(da2[th][0][:, :], da_sb[0][:, :],
                                         AF.Copy, scale=2.0)
                            e.activation(da2[th][1][:, :], da_sb[1][:, :],
                                         AF.Copy, scale=2.0).then_inc(sS.h, 1)
                        b.on("scalar", dacast)
                        da2done = sS.inc(1)

                        prev_th["pe"] = fbdone[43]
                        prev_th["sc"] = da2done
                        prev_th["vp"] = ag_mg[(43, 1, 1)]
                        prev_th["pp"] = da_ok[(43, 1)]
                        prev_th["mx"] = mxst[(43, 1)]

                    _phaseFB(bFB)
                    bFB.emit()

                fstop = fstop or (BUILD_UPTO == "FB")

                with ExitStack() as f2ctx:
                    F2 = f2ctx.enter_context
                    wdr = F2(nc.sbuf_tensor("wdr", [128, 44 * 1024], BF16))
                    dBr = [F2(nc.sbuf_tensor(f"dBr{w}", [128, 1024], BF16))
                           for w in range(2)]
                    mxt = [F2(nc.sbuf_tensor(f"mxt{i}", [128, 512], BF16))
                           for i in range(2)]
                    xrb = [F2(nc.sbuf_tensor(f"xrb{i}", [128, 512], F32))
                           for i in range(2)]
                    s2t = [F2(nc.sbuf_tensor(f"s2t{i}", [128, 512], F32))
                           for i in range(2)]

                    bF2 = Blk(nc)

                    def _phaseF2(b):
                        dmm_done = {}
                        ev2 = {}
                        st2 = {}
                        mxl_done = {}
                        wd_done = {}
                        for dh in range(2):
                            def ldw2(e, dh=dh, pe_prev=prev_th["pe"],
                                     vp=prev_th["vp"], pp=prev_th["pp"],
                                     scp=prev_th["sc"],
                                     gate=dmm_done.get((dh - 1, 3), 0)):
                                if dh == 0:
                                    e.wait_ge(sT.h, pe_prev)
                                    e.wait_ge(sV.h, vp)
                                    e.wait_ge(sP.h, pp)
                                    e.wait_ge(sS.h, scp)
                                else:
                                    e.wait_ge(sT.h, gate)
                                for kh in range(44):
                                    e.dma_start(
                                        wdr[:, kh * 1024:(kh + 1) * 1024],
                                        downW_T[kh * 128:(kh + 1) * 128,
                                                dh * 1024:(dh + 1) * 1024]
                                    ).then_inc(sWD.h, 16)
                                for w in range(2):
                                    e.dma_start(
                                        dBr[w][:, :],
                                        dB_pk[w * 128:(w + 1) * 128,
                                              dh * 1024:(dh + 1) * 1024]
                                    ).then_inc(sWD.h, 16)
                            b.on("gpsimd", ldw2)
                            wd_done[dh] = sWD.inc(16 * 46)

                            for tbg in range(4):
                                th_g, tbg2 = divmod(tbg, 2)
                                for kh in range(44):
                                    m2sl = kh % 2

                                    def mxl(e, dh=dh, tbg=tbg, kh=kh, m2sl=m2sl,
                                            nmx=sMX.n,
                                            gate=dmm_done.get("m", {}).get(
                                                (dh, tbg, kh - 2), 0) or
                                                (dmm_done.get("m", {}).get(
                                                    (dh, tbg - 1, 42 + kh), 0)
                                                 if kh < 2 else 0) or
                                                (dmm_done.get("m", {}).get(
                                                    (dh - 1, 3, 42 + kh), 0)
                                                 if kh < 2 and tbg == 0 else 0)):
                                        if dh == 0 and tbg == 0 and kh == 0:
                                            e.wait_ge(sMX.h, nmx)
                                        if gate:
                                            e.wait_ge(sT.h, gate)
                                        e.dma_start(
                                            mxt[m2sl][:, :],
                                            mixed_s[kh * 128:(kh + 1) * 128,
                                                    tbg * 512:(tbg + 1) * 512]
                                        ).then_inc(sM2[m2sl].h, 16)
                                    b.on("sync", mxl)
                                    nd_m2 = sM2[m2sl].inc(16)

                                    def dpmm(e, dh=dh, tbg=tbg, kh=kh,
                                             m2sl=m2sl, nd=nd_m2,
                                             nwd=wd_done[dh],
                                             evp=ev2.get((dh, tbg - 1, 7), 0) or
                                                 (ev2.get((dh - 1, 3, 7), 0)
                                                  if tbg == 0 else 0)):
                                        e.wait_ge(sM2[m2sl].h, nd)
                                        if kh == 0:
                                            e.wait_ge(sWD.h, nwd)
                                            if evp:
                                                e.wait_ge(sV.h, evp)
                                        for tbi in range(4):
                                            for dh2 in range(2):
                                                ins = e.matmul(
                                                    pb[tbi * 2 + dh2][:, :],
                                                    mxt[m2sl][:, tbi * 128:
                                                              (tbi + 1) * 128],
                                                    wdr[:, kh * 1024 + dh2 * 512:
                                                        kh * 1024 + (dh2 + 1) * 512],
                                                    start=(kh == 0), stop=False)
                                        ins.then_inc(sT.h, 1)
                                    b.on("tensor", dpmm)
                                    dmm_done.setdefault("m", {})[(dh, tbg, kh)] = \
                                        sT.inc(1)

                                def dbmm(e, dh=dh, tbg=tbg, th_g=th_g,
                                         tbg2=tbg2,
                                         scp=prev_th["sc"] if (dh == 0 and
                                                               tbg == 0) else 0):
                                    if scp:
                                        e.wait_ge(sS.h, scp)
                                    for w in range(2):
                                        for tbi in range(4):
                                            loc = tbg2 * 4 + tbi
                                            for dh2 in range(2):
                                                e.matmul(
                                                    pb[tbi * 2 + dh2][:, :],
                                                    da2[th_g][w][:, loc * 128:
                                                                 (loc + 1) * 128],
                                                    dBr[w][:, dh2 * 512:
                                                           (dh2 + 1) * 512],
                                                    start=False, stop=(w == 1)
                                                ).then_inc(sT.h, 1)
                                b.on("tensor", dbmm)
                                dmm_done[(dh, tbg)] = sT.inc(16)

                                for ei in range(8):
                                    tbi, dh2 = divmod(ei, 2)
                                    esl = ei % 2

                                    def xrl(e, dh=dh, tbg=tbg, tbi=tbi, dh2=dh2,
                                            esl=esl,
                                            gate=ev2.get((dh, tbg, ei - 2), 0) or
                                                 (ev2.get((dh, tbg - 1, 6 + ei), 0)
                                                  if ei < 2 else 0) or
                                                 (ev2.get((dh - 1, 3, 6 + ei), 0)
                                                  if ei < 2 and tbg == 0 else 0)):
                                        if gate:
                                            e.wait_ge(sV.h, gate)
                                        e.dma_start(
                                            xrb[esl][:, :],
                                            x2owf_s[(tbg * 4 + tbi) * 128:
                                                    (tbg * 4 + tbi + 1) * 128,
                                                    dh * 1024 + dh2 * 512:
                                                    dh * 1024 + (dh2 + 1) * 512]
                                        ).then_inc(sR2[esl].h, 16)
                                    b.on("sync", xrl)
                                    nd_xr = sR2[esl].inc(16)

                                    def ev2f(e, dh=dh, tbg=tbg, tbi=tbi, dh2=dh2,
                                             esl=esl, nd=nd_xr,
                                             ndm=dmm_done[(dh, tbg)],
                                             stg=st2.get((dh, tbg, ei - 2), 0) or
                                                 (st2.get((dh, tbg - 1, 6 + ei), 0)
                                                  if ei < 2 else 0) or
                                                 (st2.get((dh - 1, 3, 6 + ei), 0)
                                                  if ei < 2 and tbg == 0 else 0)):
                                        e.wait_ge(sT.h, ndm)
                                        e.wait_ge(sR2[esl].h, nd)
                                        if stg:
                                            e.wait_ge(sS2[esl].h, stg)
                                        e.tensor_tensor(
                                            s2t[esl][:, :],
                                            pb[tbi * 2 + dh2][:, :],
                                            xrb[esl][:, :],
                                            ALU.add).then_inc(sV.h, 1)
                                    b.on("vector", ev2f)
                                    ev2[(dh, tbg, ei)] = sV.inc(1)

                                    def st2f(e, dh=dh, tbg=tbg, tbi=tbi,
                                             dh2=dh2, esl=esl,
                                             nv=ev2[(dh, tbg, ei)]):
                                        e.wait_ge(sV.h, nv)
                                        e.dma_start(
                                            s2_s[(tbg * 4 + tbi) * 128:
                                                 (tbg * 4 + tbi + 1) * 128,
                                                 dh * 1024 + dh2 * 512:
                                                 dh * 1024 + (dh2 + 1) * 512],
                                            s2t[esl][:, :]).then_inc(sS2[esl].h, 16)
                                    b.on("sync", st2f)
                                    st2[(dh, tbg, ei)] = sS2[esl].inc(16)

                    if not fstop:
                        _phaseF2(bF2)
                        bF2.emit()

                fstop = fstop or (BUILD_UPTO == "FB2")

                with ExitStack() as fcctx:
                    FC = fcctx.enter_context
                    s5 = [FC(nc.sbuf_tensor(f"s5_{i}", [128, D], F32))
                          for i in range(3)]
                    scr5 = FC(nc.sbuf_tensor("scr5", [128, D], BF16))
                    scol5 = FC(nc.sbuf_tensor("scol5", [128, 3], F32))
                    vcol5 = FC(nc.sbuf_tensor("vcol5", [128, 3], F32))
                    mcol5 = FC(nc.sbuf_tensor("mcol5", [128, 3], F32))
                    negm5 = FC(nc.sbuf_tensor("negm5", [128, 3], F32))
                    rstd5 = FC(nc.sbuf_tensor("rstd5", [128, 3], F32))
                    vln5 = FC(nc.sbuf_tensor("vln5", [128, 1], F32))
                    x5t = [FC(nc.sbuf_tensor(f"x5t{i}", [128, D], F32))
                           for i in range(2)]

                    bFC = Blk(nc)

                    def _phaseFC(b):
                        st_done = {}
                        cm_done = {}
                        fin_n = {}

                        def _emit_o(b, tf):
                            slf = tf % 3
                            sl2 = tf % 2

                            def x5f(e, tf=tf, slf=slf, sl2=sl2, nfin=fin_n[tf],
                                    stg=st_done.get(tf - 2, 0)):
                                e.wait_ge(sS.h, nfin)
                                if stg:
                                    e.wait_ge(sOUT[sl2].h, stg)
                                e.tensor_scalar(x5t[sl2][:, :], s5[slf][:, :],
                                                mcol5[:, slf:slf + 1],
                                                rstd5[:, slf:slf + 1],
                                                ALU.subtract,
                                                ALU.mult).then_inc(sV.h, 1)
                            b.on("vector", x5f)
                            cm_done[tf] = sV.inc(1)

                            def st5(e, tf=tf, sl2=sl2, nv=cm_done[tf]):
                                e.wait_ge(sV.h, nv)
                                e.dma_start(out_own[tf * 128:(tf + 1) * 128, :],
                                            x5t[sl2][:, :]).then_inc(sOUT[sl2].h, 16)
                            b.on("sync", st5)
                            st_done[tf] = sOUT[sl2].inc(16)

                        for tb in range(NTBO):
                            sl = tb % 3

                            def ld5(e, tb=tb, sl=sl,
                                    nds2=(sS2[0].n, sS2[1].n),
                                    gate=cm_done.get(tb - 3, 0)):
                                if tb < 3:
                                    e.wait_ge(sS2[0].h, nds2[0])
                                    e.wait_ge(sS2[1].h, nds2[1])
                                if gate:
                                    e.wait_ge(sV.h, gate)
                                e.dma_start(s5[sl][:, :],
                                            s2_s[tb * 128:(tb + 1) * 128, :]
                                            ).then_inc(sB5[sl].h, 16)
                            b.on("sync", ld5)
                            nd5 = sB5[sl].inc(16)

                            def sc5(e, tb=tb, sl=sl, nd5=nd5,
                                    gate=cm_done.get(tb - 3, 0)):
                                e.wait_ge(sB5[sl].h, nd5)
                                if gate:
                                    e.wait_ge(sV.h, gate)
                                e.activation(scr5[:, :], s5[sl][:, :], AF.Copy,
                                             accum_out=scol5[:, sl:sl + 1])
                                if tb >= 1:
                                    sp = (tb - 1) % 3
                                    e.activation(scr5[:, :], s5[sp][:, :],
                                                 AF.Square,
                                                 bias=negm5[:, sp:sp + 1],
                                                 accum_out=vcol5[:, sp:sp + 1])
                                else:
                                    e.activation(scr5[:, :], s5[sl][:, :],
                                                 AF.Square,
                                                 accum_out=vcol5[:, 2:3])
                                e.activation(mcol5[:, sl:sl + 1],
                                             scol5[:, sl:sl + 1],
                                             AF.Copy, scale=1.0 / D)
                                e.activation(negm5[:, sl:sl + 1],
                                             mcol5[:, sl:sl + 1],
                                             AF.Copy, scale=-1.0)
                                if tb >= 2:
                                    sf = (tb - 2) % 3
                                    e.activation(vln5[:, :], vcol5[:, sf:sf + 1],
                                                 AF.Ln, scale=1.0 / D,
                                                 bias=EPS_LN)
                                    e.activation(rstd5[:, sf:sf + 1], vln5[:, :],
                                                 AF.Exp,
                                                 scale=-0.5).then_inc(sS.h, 1)
                            b.on("scalar", sc5)
                            if tb >= 2:
                                fin_n[tb - 2] = sS.inc(1)
                                _emit_o(b, tb - 2)

                        M1 = NTBO - 1
                        M2 = NTBO - 2

                        def sc5_tail(e, gate=cm_done.get(NTBO - 4, 0)):
                            e.wait_ge(sV.h, gate)
                            e.activation(scr5[:, :], s5[M1 % 3][:, :], AF.Square,
                                         bias=negm5[:, M1 % 3:M1 % 3 + 1],
                                         accum_out=vcol5[:, M1 % 3:M1 % 3 + 1])
                            sf = M2 % 3
                            e.activation(vln5[:, :], vcol5[:, sf:sf + 1], AF.Ln,
                                         scale=1.0 / D, bias=EPS_LN)
                            e.activation(rstd5[:, sf:sf + 1], vln5[:, :], AF.Exp,
                                         scale=-0.5).then_inc(sS.h, 1)
                            e.activation(scr5[:, :], s5[M1 % 3][:, :], AF.Copy)
                            sf = M1 % 3
                            e.activation(vln5[:, :], vcol5[:, sf:sf + 1], AF.Ln,
                                         scale=1.0 / D, bias=EPS_LN)
                            e.activation(rstd5[:, sf:sf + 1], vln5[:, :], AF.Exp,
                                         scale=-0.5).then_inc(sS.h, 1)
                        b.on("scalar", sc5_tail)
                        fin_n[M2] = sS.inc(1)
                        _emit_o(b, M2)
                        fin_n[M1] = sS.inc(1)
                        _emit_o(b, M1)

                    if not fstop:
                        _phaseFC(bFC)
                        bFC.emit()
        except StopBuild:
            pass

        bZ = Blk(nc)
        final_counts = [(s.h, s.n) for s in ALL_SEMS if s.n > 0]

        def w(e, fc=tuple(final_counts)):
            for h, n in fc:
                e.wait_ge(h, n)
        bZ.on("gpsimd", w)
        bZ.emit()

    return nc, taps



def _bf16():
    import ml_dtypes
    return ml_dtypes.bfloat16


def _prep_shared(i):
    C = np.ascontiguousarray
    sh = {}
    sh["ipW_T"] = C(i["L0_attn_in_proj_w"].T)
    sh["outW_T"] = C(i["L0_attn_out_proj_w"].T)
    cw = i["L0_attn_conv_w"][:, 0, :]
    sh["cw_t"] = C(cw.reshape(48, 128, 4).transpose(1, 0, 2).reshape(128, 192))
    sh["cb_t"] = C(i["L0_attn_conv_b"].reshape(48, 128).T)
    sh["cnw_t"] = C(i["L0_attn_conv_norm_w"].reshape(48, 128).T)
    sh["nw_t"] = C(i["L0_attn_norm_w"].reshape(16, 128).T)
    sh["onw_t"] = C(i["L0_attn_out_norm_w"].reshape(32, 128).T)
    dtw = np.zeros((65, 64), np.float32)
    dtw[0:64] = i["L0_attn_dt_w"].T
    dtw[64] = i["L0_attn_dt_b"]
    sh["dtw_aug"] = dtw
    sh["alog_r"] = C(i["L0_attn_A_log"][None, :])
    sh["gwT"] = C(i["L0_ffn_gate_w"].T)
    sh["upW_T"] = C(i["L0_ffn_w_up"].T)
    sh["gateW_T"] = C(i["L0_ffn_w_gate"].T)
    sh["downW_T"] = C(i["L0_ffn_w_down"].T)
    upA = np.zeros((2048, 256), np.float32)
    gateA = np.zeros((2048, 256), np.float32)
    dA = np.zeros((HFF, 256), np.float32)
    for e in range(NE):
        w, j = divmod(e, 4)
        c0 = w * 128 + 32 * j
        upA[:, c0:c0 + RK] = i["L0_ffn_up_A"][e].T
        gateA[:, c0:c0 + RK] = i["L0_ffn_gate_A"][e].T
        dA[:, c0:c0 + RK] = i["L0_ffn_down_A"][e].T
    sh["upA_pk"] = upA
    sh["gateA_pk"] = gateA
    sh["dA_pk"] = dA
    upB = np.zeros((256, HFF), np.float32)
    gateB = np.zeros((256, HFF), np.float32)
    for e in range(NE):
        upB[e * 32:e * 32 + RK] = i["L0_ffn_up_B"][e].T
        gateB[e * 32:e * 32 + RK] = i["L0_ffn_gate_B"][e].T
    sh["upB_stk"] = upB
    sh["gateB_stk"] = gateB
    dB = np.zeros((256, D), np.float32)
    for e in range(NE):
        j2, jj = divmod(e, 4)
        dB[j2 * 128 + jj * 32: j2 * 128 + jj * 32 + RK] = i["L0_ffn_down_B"][e].T
    sh["dB_pk"] = dB
    sh["cpool"] = np.repeat(np.array([[1e-6, 1e-5, 1.0 / D, 1.0 / CONV, -0.5, -1.0,
                                      2.0, 1.0, 0.5, 1.0 / HFF, -1e30, 0.0,
                                      1.0 / DI, 3.0, 4.0, 5.0]], np.float32), 128, axis=0)
    bf = _bf16()
    sh["ones_bf"] = np.ones((128, 128), bf)
    sh["ones_f"] = np.ones((128, 128), np.float32)
    sh["ident_f"] = np.eye(128, dtype=np.float32)
    tri = (np.arange(128)[None, :] >= np.arange(128)[:, None])
    sh["triu_bf"] = tri.astype(bf)
    sh["triu_f"] = tri.astype(np.float32)
    return sh


def make_in_maps(inputs, debug_taps=()):
    x = np.ascontiguousarray(inputs["L0_attn_x"])
    sh = _prep_shared(inputs)
    in_maps = []
    for c in range(8):
        bsel = c // 2
        m = dict(sh)
        half = c % 2
        m["x_tm"] = np.ascontiguousarray(x[bsel])
        m["x_fmT"] = np.ascontiguousarray(x[bsel].T)
        m["ownoff"] = np.array([[half * TOWN * D, half * TOWN, 0, 0]], np.int32)
        in_maps.append(m)
    return in_maps


def kernel(**inputs):
    nc, taps = build_program(debug_taps=())
    in_maps = make_in_maps(inputs)
    res = run_bass_kernel_spmd(nc, in_maps, core_ids=list(range(8)))
    out = np.empty((4, L, D), np.float32)
    for c in range(8):
        bsel, half = c // 2, c % 2
        out[bsel, half * TOWN:(half + 1) * TOWN] = res.results[c]["out_own"]
    return out
